# version 06
# baseline (speedup 1.0000x reference)
import numpy as np
from contextlib import ExitStack

import concourse.bass as bass
import concourse.mybir as mybir
from concourse.bass_utils import run_bass_kernel_spmd

F32 = mybir.dt.float32
BF16 = mybir.dt.bfloat16
F8 = mybir.dt.float8e4
AF = mybir.ActivationFunctionType
ALU = mybir.AluOpType
AX = mybir.AxisListType

D = 1024
SLAB = 8
NTOK = 4096
NTILE = 32
EPS = 1e-5


class Sem:

    def __init__(self, nc, name):
        self.h = nc._ctx.enter_context(nc.semaphore(name))
        self.n = 0

    def inc(self, instr, k=1):
        instr.then_inc(self.h, k)
        self.n += k
        return self.n


class Eng:

    def __init__(self, nc, eng, name, auto=True):
        self.e = eng
        self.auto = auto
        self.sem = Sem(nc, f"es_{name}")
        self._started = False

    def __getattr__(self, name):
        def emit(*a, **kw):
            if self.auto and self._started:
                self.e.wait_ge(self.sem.h, self.sem.n)
            self._started = True
            ins = getattr(self.e, name)(*a, **kw)
            return self.sem.inc(ins)
        return emit

    def after(self, other, val=None):
        sem = other.sem if isinstance(other, Eng) else other
        self.e.wait_ge(sem.h, sem.n if val is None else val)

    def mark(self):
        return self.sem.n


class Dma:

    def __init__(self, nc, eng, name):
        self.e = eng
        self.sem = Sem(nc, f"dq_{name}")

    def start(self, out, in_, **kw):
        return self.sem.inc(self.e.dma_start(out=out, in_=in_, **kw), 16)

    def after(self, other, val=None):
        sem = other.sem if isinstance(other, Eng) else other
        self.e.wait_ge(sem.h, sem.n if val is None else val)

    def wait_all(self):
        self.e.wait_ge(self.sem.h, self.sem.n)


class Core:

    def __init__(self, nc):
        self.nc = nc
        self.V = Eng(nc, nc.vector, "v")
        self.S = Eng(nc, nc.scalar, "s")
        self.G = Eng(nc, nc.gpsimd, "g")
        self.T = Eng(nc, nc.tensor, "t", auto=False)
        self.LD = Dma(nc, nc.sync, "ld")

    def dma(self, name, eng=None):
        return Dma(self.nc, eng or self.nc.sync, name)

    def tseq(self, *fns):
        pass


def new_nc():
    nc = bass.Bass("TRN2", target_bir_lowering=False)
    nc._ctx = ExitStack()
    return nc


def sbuf(nc, name, shape, dtype):
    return nc._ctx.enter_context(nc.sbuf_tensor(name, shape, dtype))


def psum(nc, name, shape, dtype=F32):
    return nc._ctx.enter_context(nc.psum_tensor(name, shape, dtype))



RSQRT_MAGIC = 1597463007


def rsqrt_v(C, out, x, L):
    V = C.V
    n = x.shape[-1]
    xi = x.bitcast(mybir.dt.int32)
    yi = L.n1.bitcast(mybir.dt.int32)[:, 0:n]
    V.tensor_scalar(yi, xi, 1, None, ALU.arith_shift_right)
    V.tensor_tensor(yi, L.cmagic[:, 0:n], yi, ALU.subtract)
    y = L.n1[:, 0:n]
    xh = L.n2[:, 0:n]
    V.tensor_scalar(xh, x, 0.5, None, ALU.mult)
    for _ in range(2):
        V.tensor_tensor(out, y, y, ALU.mult)
        V.tensor_tensor(out, out, xh, ALU.mult)
        V.tensor_scalar(out, out, -1.0, 1.5, ALU.mult, ALU.add)
        y_new = out
        c = V.tensor_tensor(out, y, out, ALU.mult)
        if _ == 0:
            V.tensor_copy(y, out)
    return c

class LnScratch:
    _magic = {}

    def __init__(self, nc, tag="", junk=None):
        self.junk = junk if junk is not None else sbuf(nc, f"lnjunk{tag}", [128, D], BF16)
        self.mean = sbuf(nc, f"lnmean{tag}", [128, 1], F32)
        self.rstd = sbuf(nc, f"lnrstd{tag}", [128, 1], F32)
        self.s_ = sbuf(nc, f"lns{tag}", [128, 1], F32)
        self.ss = sbuf(nc, f"lnss{tag}", [128, 1], F32)
        self.n1 = sbuf(nc, f"lnn1{tag}", [128, NB], F32)
        self.n2 = sbuf(nc, f"lnn2{tag}", [128, NB], F32)
        if id(nc) not in LnScratch._magic:
            LnScratch._magic[id(nc)] = sbuf(nc, f"cmagic{tag}", [128, NB], mybir.dt.int32)
        self.cmagic = LnScratch._magic[id(nc)][:, :]
        self._magic_init = False

    def init_magic(self, C):
        if not self._magic_init:
            C.V.memset(self.cmagic, RSQRT_MAGIC)
            LnScratch.__dict__
        self._magic_init = True


def ln_stats(C, x_ap, L):
    V, S = C.V, C.S
    L.init_magic(C)
    V.tensor_reduce(L.s_[:, :], x_ap, AX.X, ALU.add)
    if getattr(L, "_ss_reader", 0):
        S.after(C.V, L._ss_reader)
    c_ss = S.activation(L.junk[:, :], x_ap, AF.Square, accum_out=L.ss[:, :])
    c_mean = V.tensor_scalar(L.mean[:, :], L.s_[:, :], 1.0 / D, None, ALU.mult)
    V.after(C.S, c_ss)
    V.tensor_scalar(L.ss[:, :], L.ss[:, :], 1.0 / D, None, ALU.mult)
    L._ss_reader = V.tensor_scalar(L.s_[:, :], L.mean[:, :], L.mean[:, :], L.ss[:, :].opt(),
                                   ALU.mult, ALU.subtract)
    V.tensor_scalar(L.s_[:, :], L.s_[:, :], -1.0, EPS, ALU.mult, ALU.add)
    c_rstd = rsqrt_v(C, L.rstd[:, :], L.s_[:, :], L)
    return c_mean, c_rstd


def build_launch_c():
    nc = new_nc()
    x3 = nc.dram_tensor("x3", [NTOK, D], F32, kind="ExternalInput").ap()
    amax8 = nc.dram_tensor("amax8", [1, 8], F32, kind="ExternalInput").ap()
    wT = nc.dram_tensor("wT", [D, D], F32, kind="ExternalInput").ap()
    bias_rep = nc.dram_tensor("bias_rep", [128, D], F32, kind="ExternalInput").ap()
    ident = nc.dram_tensor("ident", [128, 128], BF16, kind="ExternalInput").ap()
    ones_f32 = nc.dram_tensor("ones_f32", [1, 128], F32, kind="ExternalInput").ap()
    x4 = nc.dram_tensor("x4", [NTOK, D], F32, kind="ExternalOutput").ap()
    wb = nc.dram_tensor("wb", [128], F32)

    sident = sbuf(nc, "sident", [128, 128], BF16)
    sw = [sbuf(nc, f"sw{k}", [128, D], F32) for k in range(8)]
    qw = [sbuf(nc, f"qw{k}", [128, D], BF16) for k in range(8)]
    qw8 = sbuf(nc, "qw8", [128, D], F8)
    sbias = sbuf(nc, "sbias", [128, D], F32)
    sones = sbuf(nc, "sones", [1, 128], F32)
    samax = sbuf(nc, "samax", [1, 8], F32)
    swcol = sbuf(nc, "swcol", [128, 8], F32)
    swrow = sbuf(nc, "swrow", [1, 128], F32)
    sscl = sbuf(nc, "sscl", [1, 8], F32)
    bxh = sbuf(nc, "bxh", [128, 1], F32)
    bwh = sbuf(nc, "bwh", [128, 1], F32)
    bosc = sbuf(nc, "bosc", [128, 1], F32)
    xt = [sbuf(nc, f"xt{i}", [128, D], F32) for i in range(2)]
    xn = [sbuf(nc, f"xn{i}", [128, D], F32) for i in range(2)]
    q8 = [sbuf(nc, f"q8{i}", [128, D], F8) for i in range(2)]
    qb = [sbuf(nc, f"qb{i}", [128, D], BF16) for i in range(2)]
    qfm = [sbuf(nc, f"qfm{i}", [128, SLAB * 128], BF16) for i in range(2)]
    outt = [sbuf(nc, f"outt{i}", [128, D], F32) for i in range(2)]
    Ls = [LnScratch(nc, "0"), LnScratch(nc, "1")]

    pbc = psum(nc, "pbc", [128, 1], F32)
    ptr = [psum(nc, f"ptr{i}", [128, 128], BF16) for i in range(2)]
    po = [psum(nc, f"po{i}", [128, 512], F32) for i in range(2)]

    with nc.Block() as block:
        @block.sync
        def _(sync):
            C = Core(nc)
            V, S, T, LD = C.V, C.S, C.T, C.LD
            ldm = C.dma("ldm")
            stw = C.dma("stw")
            ldx = [C.dma("ldx0"), C.dma("ldx1")]
            stx = [C.dma("stx0"), C.dma("stx1")]
            t = nc.tensor

            LD.start(sident[:, :], ident[:, :])
            LD.start(sbias[:, :], bias_rep[:, :])
            LD.start(sones[:, :], ones_f32[:, :])
            LD.start(samax[:, :], amax8[:, :])
            for k in range(8):
                LD.start(sw[k][:, :], wT[128 * k:128 * (k + 1), :])
            V.after(LD.sem)
            S.after(LD.sem)

            for k in range(8):
                V.tensor_reduce(swcol[:, k:k + 1], sw[k][:, :], AX.X, ALU.max,
                                apply_absolute_value=True)
            c_wred = V.tensor_reduce(swcol[:, 0:1], swcol[:, :], AX.X, ALU.max)
            stw.after(C.V, c_wred)
            c_wb = stw.start(wb[:], swcol[:, 0:1])
            ldm.e.wait_ge(stw.sem.h, c_wb)
            c_row = ldm.start(swrow[:, :], wb[None, :])
            V.after(ldm.sem, c_row)
            V.tensor_reduce(sscl[:, 0:1], swrow[:, :], AX.X, ALU.max,
                            apply_absolute_value=True)
            V.tensor_reduce(sscl[:, 1:2], samax[:, :], AX.X, ALU.max)
            V.reciprocal(sscl[:, 6:7], sscl[:, 0:1])
            V.reciprocal(sscl[:, 7:8], sscl[:, 1:2])
            V.tensor_scalar(sscl[:, 2:3], sscl[:, 6:7], 224.0, None, ALU.mult)
            V.tensor_scalar(sscl[:, 3:4], sscl[:, 7:8], 224.0, None, ALU.mult)
            V.tensor_scalar(sscl[:, 4:5], sscl[:, 2:3], sscl[:, 3:4], None, ALU.mult)
            c_osc = V.reciprocal(sscl[:, 5:6], sscl[:, 4:5])
            for j, (srcv, dst) in enumerate(((sscl[:, 2:3], bwh), (sscl[:, 3:4], bxh),
                                             (sscl[:, 5:6], bosc))):
                t.wait_ge(C.V.sem.h, C.V.sem.n)
                c_mm = T.matmul(pbc[:, :], sones[:, :], srcv, start=True, stop=True)
                V.after(C.T, c_mm)
                V.tensor_copy(dst[:, :], pbc[:, :])
            c_bcast = V.mark()
            S.after(C.V, c_bcast)

            for k in range(8):
                if k:
                    S.after(C.V)
                S.activation(qw8[:, :], sw[k][:, :], AF.Copy, scale=bwh[:, :].opt())
                V.after(C.S)
                V.tensor_copy(qw[k][:, :], qw8[:, :])
            c_qw = V.mark()
            t.wait_ge(C.V.sem.h, c_qw)

            resid_done = {}
            store_done = {}
            quantD = {}
            transpD = {}
            epiD = {}
            evac_done = {}

            def stage_A(i):
                p = i % 2
                buf = xt[p]
                lq = ldx[p]
                if i >= 2:
                    lq.after(C.V, resid_done[i - 2])
                c_ld = lq.start(buf[:, :], x3[128 * i:128 * (i + 1), :])
                V.after(lq.sem, c_ld)
                S.after(lq.sem, c_ld)
                c_mean, c_rstd = ln_stats(C, buf[:, :], Ls[p])
                if i >= 2:
                    V.after(C.S, quantD[i - 2])
                c_xn = V.tensor_scalar(xn[p][:, :], buf[:, :], Ls[p].mean[:, :],
                                       Ls[p].rstd[:, :], ALU.subtract, ALU.mult)
                S.after(C.V, c_xn)
                quantD[i] = S.activation(q8[p][:, :], xn[p][:, :], AF.Copy,
                                         scale=bxh[:, :].opt())
                V.after(C.S, quantD[i])
                if i >= 2:
                    V.after(C.T, transpD[i - 2])
                return V.tensor_copy(qb[p][:, :], q8[p][:, :])

            def stage_B(i, c_qb):
                p = i % 2
                t.wait_ge(C.V.sem.h, c_qb)
                for s in range(SLAB):
                    pt = ptr[s % 2]
                    if (i, s - 2) in evac_done:
                        t.wait_ge(C.S.sem.h, evac_done[(i, s - 2)])
                    elif i >= 1 and s < 2:
                        t.wait_ge(C.S.sem.h, evac_done[(i - 1, s + SLAB - 2)])
                    c_tr = T.transpose(pt[:, :], qb[p][:, 128 * s:128 * (s + 1)], sident[:, :])
                    S.after(C.T, c_tr)
                    evac_done[(i, s)] = S.copy(qfm[p][:, 128 * s:128 * (s + 1)], pt[:, :])
                transpD[i] = C.T.sem.n

            def stage_C(i):
                p = i % 2
                buf = xt[p]
                t.wait_ge(C.S.sem.h, evac_done[(i, SLAB - 1)])
                if i >= 1:
                    t.wait_ge(C.S.sem.h, epiD[i - 1])
                ins = None
                for h in range(2):
                    for k in range(8):
                        ins = nc.tensor.matmul(po[h][:, :], qfm[p][:, 128 * k:128 * (k + 1)],
                                               qw[k][:, 512 * h:512 * (h + 1)],
                                               start=(k == 0), stop=(k == 7))
                c_mm2 = C.T.sem.inc(ins)
                S.after(C.T, c_mm2)
                ot = outt[p]
                if i >= 2:
                    S.after(stx[p].sem, store_done[i - 2])
                for h in range(2):
                    S.activation(ot[:, 512 * h:512 * (h + 1)], po[h][:, :],
                                 AF.Copy, scale=bosc[:, :].opt())
                epiD[i] = S.mark()
                V.after(C.S, epiD[i])
                V.tensor_tensor(ot[:, :], ot[:, :], sbias[:, :], ALU.add)
                resid_done[i] = V.tensor_tensor(ot[:, :], ot[:, :], buf[:, :], ALU.add)
                stx[p].after(C.V, resid_done[i])
                store_done[i] = stx[p].start(x4[128 * i:128 * (i + 1), :], ot[:, :])

            c_qb0 = stage_A(0)
            stage_B(0, c_qb0)
            for i in range(NTILE):
                if i + 1 < NTILE:
                    c_qbn = stage_A(i + 1)
                stage_C(i)
                if i + 1 < NTILE:
                    stage_B(i + 1, c_qbn)
            stx[0].wait_all()
            stx[1].wait_all()
    return nc



def _bf16(x):
    import ml_dtypes
    return x.astype(ml_dtypes.bfloat16)


def prep_c_inputs(x3_full, amax_partials, weight, bias):
    wT = np.ascontiguousarray(np.asarray(weight).T)
    bias_rep = np.ascontiguousarray(np.broadcast_to(np.asarray(bias)[None, :], (128, D)))
    ident = _bf16(np.eye(128, dtype=np.float32))
    ones = np.ones((1, 128), dtype=np.float32)
    amax8 = np.asarray(amax_partials, dtype=np.float32).reshape(1, 8)
    maps = []
    for c in range(8):
        x3c = np.ascontiguousarray(
            np.asarray(x3_full)[:, 512 * c:512 * (c + 1), :].reshape(NTOK, D))
        maps.append(dict(x3=x3c, amax8=amax8, wT=wT, bias_rep=bias_rep,
                         ident=ident, ones_f32=ones))
    return maps


NB = 128
BLK = 8



def g_reduce8(G, dst3, src4):
    G.tensor_tensor(src4[:, :, :, 0:4] if src4.ndim == 4 else src4[:, :, 0:4],
                    src4[:, :, :, 0:4] if src4.ndim == 4 else src4[:, :, 0:4],
                    src4[:, :, :, 4:8] if src4.ndim == 4 else src4[:, :, 4:8], ALU.add)
    G.tensor_tensor(src4[:, :, :, 0:2] if src4.ndim == 4 else src4[:, :, 0:2],
                    src4[:, :, :, 0:2] if src4.ndim == 4 else src4[:, :, 0:2],
                    src4[:, :, :, 2:4] if src4.ndim == 4 else src4[:, :, 2:4], ALU.add)
    return G.tensor_tensor(dst3,
                           src4[:, :, :, 0:1] if src4.ndim == 4 else src4[:, :, 0:1],
                           src4[:, :, :, 1:2] if src4.ndim == 4 else src4[:, :, 1:2], ALU.add)


def build_launch_b():
    nc = new_nc()
    x2 = nc.dram_tensor("x2", [NTOK, D], F32, kind="ExternalInput").ap()
    wkT = nc.dram_tensor("wkT", [D, D], F32, kind="ExternalInput").ap()
    wvT = nc.dram_tensor("wvT", [D, D], F32, kind="ExternalInput").ap()
    wqT = nc.dram_tensor("wqT", [D, D], F32, kind="ExternalInput").ap()
    ident = nc.dram_tensor("ident", [128, 128], BF16, kind="ExternalInput").ap()
    x3 = nc.dram_tensor("x3", [NTOK, D], F32, kind="ExternalOutput").ap()
    amax1 = nc.dram_tensor("amax1", [1, 1], F32, kind="ExternalOutput").ap()
    gb = nc.dram_tensor("gb", [128], F32)

    sident = sbuf(nc, "sident", [128, 128], BF16)
    wk = [sbuf(nc, f"wk{k}", [128, D], BF16) for k in range(8)]
    wv = [sbuf(nc, f"wv{k}", [128, D], BF16) for k in range(8)]
    wq = [sbuf(nc, f"wq{k}", [128, D], BF16) for k in range(8)]
    xt = [sbuf(nc, f"xt{i}", [128, D], F32) for i in range(2)]
    xr = [sbuf(nc, f"xr{i}", [128, D], F32) for i in range(2)]
    xnb1 = sbuf(nc, "xnb1", [128, D], BF16)
    xnb = [xnb1, xnb1]
    xfm = sbuf(nc, "xfm", [128, SLAB * 128], BF16)
    kvqall = sbuf(nc, "kvqall", [128, 8 * 3 * D], BF16)
    nrm8 = sbuf(nc, "nrm8", [128, 8 * NB], F32)
    rkn8 = sbuf(nc, "rkn8", [128, 8 * NB], F32)
    S8 = sbuf(nc, "S8", [128, NB * BLK * BLK], F32)
    tmp8 = sbuf(nc, "tmp8", [128, NB * BLK * BLK], F32)
    kn = sbuf(nc, "kn", [128, D], BF16)
    retr = sbuf(nc, "retr", [128, D], F32)
    delta = sbuf(nc, "delta", [128, D], F32)
    sq_ = nrm8
    sil = sbuf(nc, "sil", [128, D], F32)
    sc8 = sil
    tmax = sbuf(nc, "tmax", [128, 1], F32)
    gmax = sbuf(nc, "gmax", [128, 1], F32)
    grow = sbuf(nc, "grow", [1, 128], F32)
    ga = sbuf(nc, "ga", [1, 1], F32)
    L = LnScratch(nc, junk=xfm)

    ptr = [psum(nc, f"ptr{i}", [128, 128], BF16) for i in range(2)]
    pk = [psum(nc, f"pk{i}", [128, 512], F32) for i in range(4)]

    FV = 4864
    NBV = FV // 64
    JV = NBV * BLK

    def s4v(ap):
        return ap[:, 0:FV].rearrange("p (nb i j) -> p nb i j", nb=NBV, i=BLK, j=BLK)

    def s4g(ap):
        return ap[:, FV:].rearrange("p (nb i j) -> p nb i j", nb=NB - NBV, i=BLK, j=BLK)

    def v3v(ap):
        return ap[:, 0:JV].rearrange("p (nb j) -> p nb j", nb=NBV, j=BLK)

    def v3g(ap):
        return ap[:, JV:].rearrange("p (nb j) -> p nb j", nb=NB - NBV, j=BLK)

    with nc.Block() as block:
        @block.sync
        def _(sync):
            C = Core(nc)
            V, S, G, T, LD = C.V, C.S, C.G, C.T, C.LD
            GD = Dma(nc, nc.gpsimd, "gld")
            lx = [C.dma("lx0"), C.dma("lx1")]
            lr = [C.dma("lr0"), C.dma("lr1")]
            sx = [C.dma("sx0"), C.dma("sx1")]
            stw = C.dma("stw")
            ldm = C.dma("ldm")
            t = nc.tensor

            LD.start(sident[:, :], ident[:, :])
            for k in range(8):
                GD.start(wk[k][:, :], wkT[128 * k:128 * (k + 1), :])
                GD.start(wv[k][:, :], wvT[128 * k:128 * (k + 1), :])
                GD.start(wq[k][:, :], wqT[128 * k:128 * (k + 1), :])
            V.after(LD.sem)
            c_init = V.memset(gmax[:, :], 0.0)

            last_store = {}
            last_xload = {}
            last_rload = {}
            tail_jobs = []
            amax_s = {}

            def emit_tail(tstep, par, ti, qt, c_tanh):
                if par in last_rload:
                    lr[par].after(C.V, last_rload[par])
                if par in amax_s:
                    lr[par].after(C.S, amax_s[par])
                if par in last_store:
                    lr[par].after(sx[par].sem, last_store[par])
                c_xr = lr[par].start(xr[par][:, :], x2[128 * ti:128 * (ti + 1), :])
                q_bv = v3v(qt).unsqueeze(2).broadcast_to([128, NBV, BLK, BLK])
                q_bg = v3g(qt).unsqueeze(2).broadcast_to([128, NB - NBV, BLK, BLK])
                V.after(C.S, c_tanh)
                G.after(C.S, c_tanh)
                V.tensor_tensor(s4v(tmp8), s4v(S8), q_bv, ALU.mult)
                c_sv = V.tensor_reduce(v3v(sq_), s4v(tmp8), AX.X, ALU.add)
                G.tensor_tensor(s4g(tmp8), s4g(S8), q_bg, ALU.mult)
                c_sg = g_reduce8(G, v3g(sq_).unsqueeze(3), s4g(tmp8))
                S.after(C.V, c_sv)
                S.after(C.G, c_sg)
                c_sil = S.activation(sil[:, :], sq_[:, :], AF.Silu)
                G.after(C.S, c_sil)
                c_om = G.tensor_tensor(sil[:, :], sq_[:, :], sil[:, :], ALU.mult)
                if par in last_store:
                    V.after(sx[par].sem, last_store[par])
                V.after(lr[par].sem, c_xr)
                V.after(C.G, c_om)
                c_x3 = V.tensor_tensor(xr[par][:, :], xr[par][:, :], sil[:, :], ALU.add)
                last_rload[par] = c_x3
                S.after(C.V, c_x3)
                c_m2, c_r2 = ln_stats(C, xr[par][:, :], L)
                V.tensor_scalar(sc8[:, :], xr[par][:, :], L.mean[:, :], L.rstd[:, :],
                                ALU.subtract, ALU.mult)
                V.tensor_reduce(tmax[:, :], sc8[:, :], AX.X, ALU.max,
                                apply_absolute_value=True)
                c_gm = V.tensor_tensor(gmax[:, :], gmax[:, :], tmax[:, :], ALU.max)
                sx[par].after(C.V, c_x3)
                last_store[par] = sx[par].start(x3[128 * ti:128 * (ti + 1), :], xr[par][:, :])
                last_rload[par] = c_gm
                amax_s[par] = C.S.sem.n

            for rt in range(4):
                while tail_jobs:
                    emit_tail(*tail_jobs.pop(0))
                V.after(C.S)
                V.after(C.G)
                c_zS = V.memset(S8[:, :], 0.0)
                G.after(C.V, c_zS)

                proj_evac = {}
                prev_tr = None
                for tstep in range(8):
                    ti = tstep * 4 + rt
                    par = tstep % 2
                    if par in last_xload:
                        lx[par].after(C.V, last_xload[par])
                    c_ld = lx[par].start(xt[par][:, :], x2[128 * ti:128 * (ti + 1), :])
                    V.after(lx[par].sem, c_ld)
                    S.after(lx[par].sem, c_ld)
                    c_mean, c_rstd = ln_stats(C, xt[par][:, :], L)
                    if prev_tr is not None:
                        V.after(C.T, prev_tr)
                    c_xnb = V.tensor_scalar(xnb[par][:, :], xt[par][:, :], L.mean[:, :],
                                            L.rstd[:, :], ALU.subtract, ALU.mult)
                    last_xload[par] = c_xnb
                    t.wait_ge(C.V.sem.h, c_xnb)
                    ev = {}
                    for s in range(SLAB):
                        pt = ptr[s % 2]
                        if s >= 2:
                            t.wait_ge(C.S.sem.h, ev[s - 2])
                        c_tr = T.transpose(pt[:, :], xnb[par][:, 128 * s:128 * (s + 1)],
                                           sident[:, :])
                        S.after(C.T, c_tr)
                        ev[s] = S.copy(xfm[:, 128 * s:128 * (s + 1)], pt[:, :])
                    prev_tr = c_tr
                    t.wait_ge(C.S.sem.h, ev[SLAB - 1])
                    if rt == 0 and tstep == 0:
                        t.wait_ge(GD.sem.h, GD.sem.n)
                    for wi, wmat in enumerate((wk, wv, wq)):
                        for h in range(2):
                            gidx = wi * 2 + h
                            pp = pk[gidx % 4]
                            key = (tstep, gidx)
                            prev = proj_evac.get((tstep, gidx - 4)) or \
                                proj_evac.get((tstep - 1, gidx + 2))
                            if prev:
                                t.wait_ge(C.S.sem.h, prev)
                            for kc in range(8):
                                ins = t.matmul(pp[:, :], xfm[:, 128 * kc:128 * (kc + 1)],
                                               wmat[kc][:, 512 * h:512 * (h + 1)],
                                               start=(kc == 0), stop=(kc == 7))
                            c_mm = C.T.sem.inc(ins)
                            S.after(C.T, c_mm)
                            proj_evac[key] = S.copy(
                                kvqall[:, 3072 * tstep + D * wi + 512 * h:
                                       3072 * tstep + D * wi + 512 * (h + 1)], pp[:, :])
                    kt = kvqall[:, 3072 * tstep:3072 * tstep + D]
                    V.after(C.S, proj_evac[(tstep, 1)])
                    V.tensor_tensor(retr[:, :], kt, kt, ALU.mult)
                    V.tensor_reduce(nrm8[:, NB * tstep:NB * (tstep + 1)],
                                    retr[:, :].rearrange("p (nb j) -> p nb j", nb=NB, j=BLK),
                                    AX.X, ALU.add)
                c_n8 = V.mark()
                S.after(C.V, c_n8)
                c_sq8 = S.activation(nrm8[:, :], nrm8[:, :], AF.Sqrt)
                V.after(C.S, c_sq8)
                V.tensor_scalar(nrm8[:, :], nrm8[:, :], 1e-6, None, ALU.add)
                c_rk8 = V.reciprocal(rkn8[:, :], nrm8[:, :])
                G.after(C.V, c_rk8)

                for tstep in range(8):
                    ti = tstep * 4 + rt
                    par = tstep % 2
                    kt = kvqall[:, 3072 * tstep:3072 * tstep + D]
                    vt = kvqall[:, 3072 * tstep + D:3072 * tstep + 2 * D]
                    qt = kvqall[:, 3072 * tstep + 2 * D:3072 * tstep + 3 * D]
                    rk = rkn8[:, NB * tstep:NB * (tstep + 1)]
                    if tail_jobs:
                        V.after(C.S, tail_jobs[-1][-1])
                        G.after(C.S, tail_jobs[-1][-1])
                    c_knv = V.tensor_tensor(
                        v3v(kn), v3v(kt),
                        rk[:, 0:NBV].unsqueeze(2).broadcast_to([128, NBV, BLK]), ALU.mult)
                    c_kng = G.tensor_tensor(
                        v3g(kn), v3g(kt),
                        rk[:, NBV:].unsqueeze(2).broadcast_to([128, NB - NBV, BLK]), ALU.mult)
                    kn_bv = v3v(kn).unsqueeze(2).broadcast_to([128, NBV, BLK, BLK])
                    kn_bg = v3g(kn).unsqueeze(2).broadcast_to([128, NB - NBV, BLK, BLK])
                    V.tensor_tensor(s4v(tmp8), s4v(S8), kn_bv, ALU.mult)
                    c_rv = V.tensor_reduce(v3v(retr), s4v(tmp8), AX.X, ALU.add)
                    G.tensor_tensor(s4g(tmp8), s4g(S8), kn_bg, ALU.mult)
                    c_rg = g_reduce8(G, v3g(retr).unsqueeze(3), s4g(tmp8))
                    c_dv = V.tensor_tensor(delta[:, 0:JV], vt[:, 0:JV], retr[:, 0:JV],
                                           ALU.subtract)
                    c_dg = G.tensor_tensor(delta[:, JV:], vt[:, JV:], retr[:, JV:],
                                           ALU.subtract)
                    d_bv = v3v(delta).unsqueeze(3).broadcast_to([128, NBV, BLK, BLK])
                    d_bg = v3g(delta).unsqueeze(3).broadcast_to([128, NB - NBV, BLK, BLK])
                    if tail_jobs:
                        emit_tail(*tail_jobs.pop(0))
                    V.tensor_tensor(s4v(tmp8), d_bv, kn_bv, ALU.mult)
                    c_av = V.tensor_tensor(S8[:, 0:FV], S8[:, 0:FV], tmp8[:, 0:FV], ALU.add)
                    G.tensor_tensor(s4g(tmp8), d_bg, kn_bg, ALU.mult)
                    c_ag = G.tensor_tensor(S8[:, FV:], S8[:, FV:], tmp8[:, FV:], ALU.add)
                    S.after(C.V, c_av)
                    S.after(C.G, c_ag)
                    c_tanh = S.activation(S8[:, :], S8[:, :], AF.Tanh)
                    tail_jobs.append((tstep, par, ti, qt, c_tanh))

            while tail_jobs:
                emit_tail(*tail_jobs.pop(0))
            stw.after(C.V)
            c_gb = stw.start(gb[:], gmax[:, :])
            ldm.e.wait_ge(stw.sem.h, c_gb)
            c_grow = ldm.start(grow[:, :], gb[None, :])
            V.after(ldm.sem, c_grow)
            c_ga = V.tensor_reduce(ga[:, :], grow[:, :], AX.X, ALU.max)
            stw.after(C.V, c_ga)
            stw.start(amax1[:, :], ga[:, :])
            sx[0].wait_all()
            sx[1].wait_all()
            stw.wait_all()
    return nc


def prep_b_inputs(x2_full, W_k, W_v, W_q):
    ident = _bf16(np.eye(128, dtype=np.float32))
    wkT = np.ascontiguousarray(np.asarray(W_k).T)
    wvT = np.ascontiguousarray(np.asarray(W_v).T)
    wqT = np.ascontiguousarray(np.asarray(W_q).T)
    maps = []
    for c in range(8):
        x2c = np.ascontiguousarray(
            np.asarray(x2_full)[:, 512 * c:512 * (c + 1), :].reshape(NTOK, D))
        maps.append(dict(x2=x2c, wkT=wkT, wvT=wvT, wqT=wqT, ident=ident))
    return maps



def push_scope(nc):
    s = ExitStack()
    nc._scopes.append(s)
    return s


def pop_scope(nc):
    nc._scopes.pop().close()


def sbuf2(nc, name, shape, dtype):
    st = nc._scopes[-1] if getattr(nc, "_scopes", None) else nc._ctx
    return st.enter_context(nc.sbuf_tensor(name, shape, dtype))


def ln_transpose_tile(C, nc, buf, xnb, xfm_dst_cols, L, sident, ptr, prev_xnb_reader):
    V, S, T = C.V, C.S, C.T
    t = nc.tensor
    c_mean, c_rstd = ln_stats(C, buf[:, :], L)
    if prev_xnb_reader is not None:
        V.after(C.T, prev_xnb_reader)
    c_xnb = V.tensor_scalar(xnb[:, :], buf[:, :], L.mean[:, :], L.rstd[:, :],
                            ALU.subtract, ALU.mult)
    t.wait_ge(C.V.sem.h, c_xnb)
    ev = {}
    c_tr = None
    for s in range(SLAB):
        pt = ptr[s % 2]
        if s >= 2:
            t.wait_ge(C.S.sem.h, ev[s - 2])
        c_tr = T.transpose(pt, xnb[:, 128 * s:128 * (s + 1)], sident[:, :])
        S.after(C.T, c_tr)
        ev[s] = S.copy(xfm_dst_cols(s), pt[:, :])
    return c_tr, ev[SLAB - 1]


def build_launch_a():
    nc = new_nc()
    nc._scopes = []
    x0 = nc.dram_tensor("x0", [NTOK, D], F32, kind="ExternalInput").ap()
    Wq = nc.dram_tensor("Wq", [D, D], F32, kind="ExternalInput").ap()
    Wk = nc.dram_tensor("Wk", [D, D], F32, kind="ExternalInput").ap()
    Wv = nc.dram_tensor("Wv", [D, D], F32, kind="ExternalInput").ap()
    Wo = nc.dram_tensor("Wo", [D, D], F32, kind="ExternalInput").ap()
    Wg1 = nc.dram_tensor("Wg1", [D, 64], F32, kind="ExternalInput").ap()
    Wg2 = nc.dram_tensor("Wg2", [64, D], F32, kind="ExternalInput").ap()
    lnw = nc.dram_tensor("lnw", [128, SLAB], F32, kind="ExternalInput").ap()
    lnb = nc.dram_tensor("lnb", [128, SLAB], F32, kind="ExternalInput").ap()
    cosd = nc.dram_tensor("cosd", [128, NTOK], F32, kind="ExternalInput").ap()
    nsind = nc.dram_tensor("nsind", [128, NTOK], F32, kind="ExternalInput").ap()
    w1T = nc.dram_tensor("w1T", [D, 4096], F32, kind="ExternalInput").ap()
    w2T = nc.dram_tensor("w2T", [D, 4096], F32, kind="ExternalInput").ap()
    h1T = nc.dram_tensor("h1T", [4096, 4096], F32, kind="ExternalInput").ap()
    h2T = nc.dram_tensor("h2T", [4096, 4096], F32, kind="ExternalInput").ap()
    w3T = nc.dram_tensor("w3T", [4096, D], F32, kind="ExternalInput").ap()
    s1r = nc.dram_tensor("s1r", [128, 32], F32, kind="ExternalInput").ap()
    s2r = nc.dram_tensor("s2r", [128, 32], F32, kind="ExternalInput").ap()
    sh1r = nc.dram_tensor("sh1r", [128, 32], F32, kind="ExternalInput").ap()
    sh2r = nc.dram_tensor("sh2r", [128, 32], F32, kind="ExternalInput").ap()
    s3rep = nc.dram_tensor("s3rep", [128, D], F32, kind="ExternalInput").ap()
    ident = nc.dram_tensor("ident", [128, 128], BF16, kind="ExternalInput").ap()
    maskT = nc.dram_tensor("maskT", [128, 2048], BF16, kind="ExternalInput").ap()
    onesb = nc.dram_tensor("onesb", [128, 1], BF16, kind="ExternalInput").ap()
    onesf = nc.dram_tensor("onesf", [1, 128], F32, kind="ExternalInput").ap()
    x1out = nc.dram_tensor("x1", [NTOK, D], F32, kind="ExternalOutput").ap()
    x2out = nc.dram_tensor("x2", [NTOK, D], F32, kind="ExternalOutput").ap()
    w1scr = nc.dram_tensor("w1scr", [32, 128, 1024], BF16).ap()
    w2scr = nc.dram_tensor("w2scr", [32, 128, 1024], BF16).ap()
    h1scr = nc.dram_tensor("h1scr", [32, 128, 4096], BF16).ap()
    h2scr = nc.dram_tensor("h2scr", [32, 128, 4096], BF16).ap()
    w3scr = nc.dram_tensor("w3scr", [32, 128, 1024], BF16).ap()

    sident = sbuf(nc, "sident", [128, 128], BF16)
    smask = sbuf(nc, "smask", [128, 2048], BF16)
    slnw = sbuf(nc, "slnw", [128, SLAB], F32)
    slnb = sbuf(nc, "slnb", [128, SLAB], F32)
    sonesb = sbuf(nc, "sonesb", [128, 1], BF16)
    sonesf = sbuf(nc, "sonesf", [1, 128], F32)
    L = LnScratch(nc)

    px = [psum(nc, f"px{i}", [128, 512], F32) for i in range(2)]
    ptr = [p[:].bitcast(BF16)[:, 0:128] for p in px]
    pk = [psum(nc, f"pk{i}", [128, 512], F32) for i in range(4)]
    pout = psum(nc, "pout", [128, 128], F32)
    pstate = psum(nc, "pstate", [128, 64], F32)

    with nc.Block() as block:
        @block.sync
        def _(sync):
            C = Core(nc)
            V, S, G, T, LD = C.V, C.S, C.G, C.T, C.LD
            GD = Dma(nc, nc.gpsimd, "gld")
            lx = [C.dma("lx0"), C.dma("lx1")]
            lx4 = [C.dma(f"lx4_{i}") for i in range(4)]
            sx4 = [C.dma(f"sx4_{i}") for i in range(4)]
            sx = [C.dma("sx0"), C.dma("sx1")]
            lrope = C.dma("lrope")
            t = nc.tensor

            LD.start(sident[:, :], ident[:, :])
            LD.start(smask[:, :], maskT[:, :])
            LD.start(slnw[:, :], lnw[:, :])
            LD.start(slnb[:, :], lnb[:, :])
            LD.start(sonesb[:, :], onesb[:, :])
            LD.start(sonesf[:, :], onesf[:, :])
            V.after(LD.sem)
            S.after(LD.sem)
            T.e.wait_ge(LD.sem.h, LD.sem.n)

            push_scope(nc)
            wq = [sbuf2(nc, f"wq{k}", [128, D], BF16) for k in range(8)]
            wk = [sbuf2(nc, f"wk{k}", [128, D], BF16) for k in range(8)]
            wv = [sbuf2(nc, f"wv{k}", [128, D], BF16) for k in range(8)]
            wo = [sbuf2(nc, f"wo{k}", [128, D], BF16) for k in range(8)]
            wg1 = sbuf2(nc, "wg1", [128, SLAB * 64], BF16)
            wg2 = sbuf2(nc, "wg2", [64, D], BF16)
            scos = [sbuf2(nc, f"scos{i}", [128, 512], F32) for i in range(2)]
            snsin = [sbuf2(nc, f"snsin{i}", [128, 512], F32) for i in range(2)]
            x0t = [sbuf2(nc, f"x0t{i}", [128, D], F32) for i in range(4)]
            xnb = sbuf2(nc, "xnb", [128, D], BF16)
            xfm = sbuf2(nc, "xfm", [128, SLAB * 512], BF16)
            qraw = [sbuf2(nc, f"qraw{i}", [128, 512], F32) for i in range(2)]
            rtmp = sbuf2(nc, "rtmp", [128, 512], F32)
            rcr = sbuf2(nc, "rcr", [128, 512], F32)
            Qr = sbuf2(nc, "Qr", [128, SLAB * 512], BF16)
            Kr = sbuf2(nc, "Kr", [128, SLAB * 512], BF16)
            ktm = sbuf2(nc, "ktm", [128, 4 * D], BF16)
            vtm = sbuf2(nc, "vtm", [128, 4 * D], BF16)
            Sf = sbuf2(nc, "Sf", [128, SLAB * 64], F32)
            Sb = sbuf2(nc, "Sb", [128, SLAB * 64], BF16)
            eA = sbuf2(nc, "eA", [128, 512], BF16)
            eB = sbuf2(nc, "eB", [128, 512], BF16)
            attn = sbuf2(nc, "attn", [128, SLAB * 512], BF16)
            gate = sbuf2(nc, "gate", [128, SLAB * 512], BF16)
            g1sb = sbuf2(nc, "g1sb", [64, 512], BF16)
            attq = sbuf2(nc, "attq", [128, SLAB * 512], BF16)
            srow = sbuf2(nc, "srow", [1, 512], F32)
            sqrow = sbuf2(nc, "sqrow", [1, 512], F32)
            murow = sbuf2(nc, "murow", [1, 512], F32)
            rsrow = sbuf2(nc, "rsrow", [1, 512], F32)
            mub = sbuf2(nc, "mub", [128, 512], F32)
            rsb = sbuf2(nc, "rsb", [128, 512], F32)
            nrmd = sbuf2(nc, "nrmd", [128, SLAB * 512], BF16)

            for k in range(8):
                GD.start(wq[k][:, :], Wq[128 * k:128 * (k + 1), :])
                GD.start(wk[k][:, :], Wk[128 * k:128 * (k + 1), :])
                GD.start(wv[k][:, :], Wv[128 * k:128 * (k + 1), :])
                GD.start(wo[k][:, :], Wo[128 * k:128 * (k + 1), :])
                GD.start(wg1[:, 64 * k:64 * (k + 1)], Wg1[128 * k:128 * (k + 1), :])
            GD.start(wg2[:, :], Wg2[:, :])
            V.memset(Sf[:, :], 0.0)
            V.memset(Sb[:, :], 0.0)

            wstage = [sbuf2(nc, f"wst{i}", [128, 2048], BF16) for i in range(2)]
            gcv = Sem(nc, "gcv")
            gcs2 = [Sem(nc, "gcs0"), Sem(nc, "gcs1")]
            conv_jobs = []
            for o in range(32):
                conv_jobs.append((w1T[:, 128 * o:128 * (o + 1)]
                                  .rearrange("(kc p) c -> p kc c", p=128),
                                  w1scr[o], 1024))
            for o in range(32):
                conv_jobs.append((w2T[:, 128 * o:128 * (o + 1)]
                                  .rearrange("(kc p) c -> p kc c", p=128),
                                  w2scr[o], 1024))
            for o in range(32):
                conv_jobs.append((w3T[128 * o:128 * (o + 1), :], w3scr[o], 1024))
            for hT, hscr in ((h1T, h1scr), (h2T, h2scr)):
                for o in range(32):
                    for half in range(2):
                        conv_jobs.append((
                            hT[2048 * half:2048 * (half + 1), 128 * o:128 * (o + 1)]
                            .rearrange("(kc p) c -> p kc c", p=128),
                            hscr[o][:, 2048 * half:2048 * (half + 1)], 2048))
            g_ = nc.gpsimd
            store_hist = []
            for i, (srcap, dstap, width) in enumerate(conv_jobs):
                stg = wstage[i % 2]
                if i >= 2:
                    g_.wait_ge(gcs2[i % 2].h, store_hist[i - 2])
                ldap = stg[:, 0:width]
                if width == 1024:
                    ldap = stg[:, 0:1024]
                if srcap.ndim == 3:
                    ldap = ldap.rearrange("p (kc c) -> p kc c", kc=srcap.shape[1])
                c_l = gcv.inc(g_.dma_start(out=ldap, in_=srcap), 16)
                g_.wait_ge(gcv.h, c_l)
                store_hist.append(gcs2[i % 2].inc(g_.dma_start(out=dstap, in_=stg[:, 0:width]), 16))

            rope_used = {}
            x0_used = {}
            x1_last = {}
            prev_mask_last = 0
            attn_evac_last = 0
            state_prev_slab = 0
            for g in range(8):
                if g >= 2:
                    lrope.after(C.V, rope_used[g - 2])
                lrope.start(scos[g % 2][:, :], cosd[:, 512 * g:512 * (g + 1)])
                c_rt = lrope.start(snsin[g % 2][:, :], nsind[:, 512 * g:512 * (g + 1)])
                prev_rd = None
                for til in range(4):
                    ti = 4 * g + til
                    buf = x0t[til]
                    if g >= 1:
                        lx4[til].after(C.V, x0_used[(g - 1, til)])
                        lx4[til].after(sx4[til].sem, x1_last[(g - 1, til)])
                    c_ld = lx4[til].start(buf[:, :], x0[128 * ti:128 * (ti + 1), :])
                    V.after(lx4[til].sem, c_ld)
                    S.after(lx4[til].sem, c_ld)
                    prev_rd, c_lastev = ln_transpose_tile(
                        C, nc, buf, xnb,
                        lambda s, _t=til: xfm[:, 512 * s + 128 * _t:512 * s + 128 * (_t + 1)],
                        L, sident, ptr,
                        prev_rd if til > 0 else (C.T.sem.n if g >= 1 else None))
                t.wait_ge(C.S.sem.h, c_lastev)
                if g == 0:
                    t.wait_ge(GD.sem.h, GD.sem.n)
                V.after(lrope.sem, c_rt)
                rope_done = {}
                pevac = {}
                for i in range(16):
                    s, isk = divmod(i, 2)
                    wmat, dst = ((wq, Qr), (wk, Kr))[isk]
                    pb = pk[i % 4]
                    if i >= 4:
                        t.wait_ge(C.S.sem.h, pevac[i - 4])
                    for kc in range(8):
                        ins = t.matmul(pb[:, :], wmat[kc][:, 128 * s:128 * (s + 1)],
                                       xfm[:, 512 * kc:512 * (kc + 1)],
                                       start=(kc == 0), stop=(kc == 7))
                    c_mm = C.T.sem.inc(ins)
                    S.after(C.T, c_mm)
                    if i >= 2:
                        S.after(C.V, rope_done[i - 2])
                    pevac[i] = S.activation(qraw[i % 2][:, :], pb[:, :], AF.Silu)
                    V.after(C.S, pevac[i])
                    q_ = qraw[i % 2]
                    V.tensor_tensor(rtmp[:, :], q_[:, :], scos[g % 2][:, :], ALU.mult)
                    for (dpo, spo) in ((0, 32), (32, 0), (64, 96), (96, 64)):
                        V.tensor_copy(rcr[dpo:dpo + 32, :], q_[spo:spo + 32, :])
                    V.tensor_tensor(rcr[:, :], rcr[:, :], snsin[g % 2][:, :], ALU.mult)
                    rope_done[i] = V.tensor_tensor(dst[:, 512 * s:512 * (s + 1)],
                                                   rtmp[:, :], rcr[:, :], ALU.add)
                t.wait_ge(C.V.sem.h, rope_done[15])
                ev = {}
                for s in range(SLAB):
                    for til in range(4):
                        j = 4 * s + til
                        pt = ptr[j % 2]
                        if j >= 2:
                            t.wait_ge(C.S.sem.h, ev[j - 2])
                        c_tr = T.transpose(pt, Kr[:, 512 * s + 128 * til:512 * s + 128 * (til + 1)],
                                           sident[:, :])
                        S.after(C.T, c_tr)
                        ev[j] = S.copy(ktm[:, 1024 * til + 128 * s:1024 * til + 128 * (s + 1)], pt[:, :])
                t.wait_ge(C.S.sem.h, ev[4 * SLAB - 1])
                vp_evac = {}
                for j in range(8):
                    til, h = divmod(j, 2)
                    pb = pk[j % 4]
                    if j >= 4:
                        t.wait_ge(C.S.sem.h, vp_evac[j - 4])
                    for kc in range(8):
                        ins = t.matmul(pb[:, :], xfm[:, 512 * kc + 128 * til:512 * kc + 128 * (til + 1)],
                                       wv[kc][:, 512 * h:512 * (h + 1)],
                                       start=(kc == 0), stop=(kc == 7))
                    c_mm = C.T.sem.inc(ins)
                    S.after(C.T, c_mm)
                    vp_evac[j] = S.copy(vtm[:, 1024 * til + 512 * h:1024 * til + 512 * (h + 1)], pb[:, :])
                t.wait_ge(C.S.sem.h, vp_evac[4])
                for kc in range(8):
                    ins = t.matmul(pk[0][0:64, :], wg1[:, 64 * kc:64 * (kc + 1)],
                                   xfm[:, 512 * kc:512 * (kc + 1)],
                                   start=(kc == 0), stop=(kc == 7))
                c_mm = C.T.sem.inc(ins)
                S.after(C.T, c_mm)
                c_g1 = S.copy(g1sb[:, :], pk[0][0:64, :])
                t.wait_ge(C.S.sem.h, c_g1)
                gate_evac = {}
                for s in range(SLAB):
                    pb = pk[1 + (s % 3)]
                    if s >= 3:
                        t.wait_ge(C.S.sem.h, gate_evac[s - 3])
                    elif s == 0:
                        t.wait_ge(C.S.sem.h, vp_evac[7])
                    ins = t.matmul(pb[:, :], wg2[:, 128 * s:128 * (s + 1)], g1sb[:, :],
                                   start=True, stop=True)
                    c_mm = C.T.sem.inc(ins)
                    S.after(C.T, c_mm)
                    gate_evac[s] = S.activation(gate[:, 512 * s:512 * (s + 1)], pk[1 + (s % 3)][:, :],
                                                AF.Sigmoid)
                c_gate_all = S.mark()
                for s in range(SLAB):
                    gcol = slice(512 * s, 512 * (s + 1))
                    mask_v = {}
                    av_t = {}
                    for kt in range(4):
                        kcol = slice(512 * s + 128 * kt, 512 * s + 128 * (kt + 1))
                        vA = slice(1024 * kt + 128 * s, 1024 * kt + 128 * s + 64)
                        vB = slice(1024 * kt + 128 * s + 64, 1024 * kt + 128 * (s + 1))
                        if kt >= 2:
                            t.wait_ge(C.V.sem.h, mask_v[kt - 2])
                        elif prev_mask_last:
                            t.wait_ge(C.V.sem.h, prev_mask_last)
                        t.matmul(px[0][:, :], Kr[0:64, kcol], Qr[0:64, gcol],
                                 start=True, stop=True, tile_position=(0, 0))
                        ins = t.matmul(px[1][:, :], Kr[64:128, kcol], Qr[64:128, gcol],
                                       start=True, stop=True, tile_position=(64, 0))
                        c_e = C.T.sem.inc(ins)
                        V.after(C.T, c_e)
                        if kt >= 1:
                            V.after(C.T, av_t[kt - 1])
                        V.tensor_tensor(eA[:, :], px[0][:, :], smask[:, 512 * kt:512 * (kt + 1)], ALU.mult)
                        mask_v[kt] = V.tensor_tensor(eB[:, :], px[1][:, :],
                                                     smask[:, 512 * kt:512 * (kt + 1)], ALU.mult)
                        if kt == 0:
                            t.wait_ge(C.S.sem.h, c_gate_all if s == 0 else attn_evac_last)
                        t.wait_ge(C.V.sem.h, mask_v[kt])
                        last_stop = (kt == 3 and g == 0)
                        t.matmul(pk[0][0:64, :], vtm[:, vA], eA[:, :],
                                 start=(kt == 0), stop=last_stop, tile_position=(0, 0))
                        ins = t.matmul(pk[1][0:64, :], vtm[:, vB], eB[:, :],
                                       start=(kt == 0), stop=last_stop, tile_position=(0, 0))
                        av_t[kt] = C.T.sem.inc(ins)
                    if g > 0:
                        t.matmul(pk[0][0:64, :], Sb[0:64, 64 * s:64 * (s + 1)],
                                 Qr[0:64, gcol], start=False, stop=True, tile_position=(0, 0))
                        ins = t.matmul(pk[1][0:64, :], Sb[64:128, 64 * s:64 * (s + 1)],
                                       Qr[64:128, gcol], start=False, stop=True,
                                       tile_position=(64, 0))
                        C.T.sem.inc(ins)
                    S.after(C.T)
                    S.copy(attn[0:64, gcol], pk[0][0:64, :])
                    attn_evac_last = S.copy(attn[64:128, gcol], pk[1][0:64, :])
                    if state_prev_slab:
                        t.wait_ge(C.V.sem.h, state_prev_slab)
                    for kt in range(4):
                        vA = slice(1024 * kt + 128 * s, 1024 * kt + 128 * s + 64)
                        t.matmul(pstate[0:64, :], ktm[:, vA], vtm[:, vA],
                                 start=(kt == 0), stop=(kt == 3), tile_position=(0, 0))
                    for kt in range(4):
                        vB = slice(1024 * kt + 128 * s + 64, 1024 * kt + 128 * (s + 1))
                        ins = t.matmul(pstate[64:128, :], ktm[:, vB], vtm[:, vB],
                                       start=(kt == 0), stop=(kt == 3), tile_position=(0, 64))
                    c_st = C.T.sem.inc(ins)
                    V.after(C.T, c_st)
                    V.tensor_tensor(Sf[:, 64 * s:64 * (s + 1)], Sf[:, 64 * s:64 * (s + 1)],
                                    pstate[:, :], ALU.add)
                    state_prev_slab = V.tensor_copy(Sb[:, 64 * s:64 * (s + 1)],
                                                    Sf[:, 64 * s:64 * (s + 1)])
                    prev_mask_last = mask_v[3]
                t.wait_ge(C.V.sem.h, state_prev_slab)
                V.after(C.S, attn_evac_last)
                c_gm = V.tensor_tensor(attn[:, :], attn[:, :], gate[:, :], ALU.mult)
                S.after(C.V, c_gm)
                c_asq = S.activation(attq[:, :], attn[:, :], AF.Square)
                t.wait_ge(C.V.sem.h, c_gm)
                t.wait_ge(C.S.sem.h, c_asq)
                for j, (src, dstrow) in enumerate(((attn, srow), (attq, sqrow))):
                    for s in range(SLAB):
                        ins = t.matmul(pk[j][0:1, :], sonesb[:, :], src[:, 512 * s:512 * (s + 1)],
                                       start=(s == 0), stop=(s == 7))
                    c_mm = C.T.sem.inc(ins)
                    V.after(C.T, c_mm)
                    V.tensor_copy(dstrow[:, :], pk[j][0:1, :])
                V.tensor_scalar(murow[:, :], srow[:, :], 1.0 / D, None, ALU.mult)
                V.tensor_scalar(sqrow[:, :], sqrow[:, :], 1.0 / D, None, ALU.mult)
                V.tensor_tensor(rsrow[:, :], murow[:, :], murow[:, :], ALU.mult)
                V.tensor_tensor(rsrow[:, :], sqrow[:, :], rsrow[:, :], ALU.subtract)
                V.tensor_scalar(rsrow[:, :], rsrow[:, :], 1.0, EPS, ALU.mult, ALU.add)
                c_v = V.reciprocal(rsrow[:, :], rsrow[:, :])
                S.after(C.V, c_v)
                c_rs = S.activation(rsrow[:, :], rsrow[:, :], AF.Sqrt)
                t.wait_ge(C.S.sem.h, c_rs)
                t.wait_ge(C.V.sem.h, C.V.sem.n)
                t.matmul(pk[2][:, :], sonesf[:, :], murow[:, :], start=True, stop=True)
                ins = t.matmul(pk[3][:, :], sonesf[:, :], rsrow[:, :], start=True, stop=True)
                c_bc = C.T.sem.inc(ins)
                V.after(C.T, c_bc)
                V.tensor_copy(mub[:, :], pk[2][:, :])
                V.tensor_copy(rsb[:, :], pk[3][:, :])
                for s in range(SLAB):
                    sl = slice(512 * s, 512 * (s + 1))
                    V.tensor_tensor(attn[:, sl], attn[:, sl], mub[:, :], ALU.subtract)
                    V.tensor_tensor(attn[:, sl], attn[:, sl], rsb[:, :], ALU.mult)
                    V.tensor_scalar(nrmd[:, sl], attn[:, sl], slnw[:, s:s + 1],
                                    slnb[:, s:s + 1], ALU.mult, ALU.add)
                c_nrmd = V.mark()
                t.wait_ge(C.V.sem.h, c_nrmd)
                for til in range(4):
                    xb = x0t[til]
                    c_r = None
                    for h in range(2):
                        pp = pk[h]
                        for kc in range(8):
                            ins = t.matmul(pp[:, :], nrmd[:, 512 * kc + 128 * til:512 * kc + 128 * (til + 1)],
                                           wo[kc][:, 512 * h:512 * (h + 1)],
                                           start=(kc == 0), stop=(kc == 7))
                        c_mm = C.T.sem.inc(ins)
                        V.after(C.T, c_mm)
                        c_r = V.tensor_tensor(xb[:, 512 * h:512 * (h + 1)], pp[:, :],
                                              xb[:, 512 * h:512 * (h + 1)], ALU.add)
                        t.wait_ge(C.V.sem.h, c_r)
                    sx4[til].after(C.V, c_r)
                    x1_last[(g, til)] = sx4[til].start(
                        x1out[128 * (4 * g + til):128 * (4 * g + til + 1), :], xb[:, :])
                    x0_used[(g, til)] = V.mark()
                rope_used[g] = V.mark()

            for q_ in sx4:
                q_.wait_all()
            for eng in (V.e, S.e, t, G.e, LD.e):
                eng.wait_ge(C.V.sem.h, C.V.sem.n)
                eng.wait_ge(C.S.sem.h, C.S.sem.n)
                eng.wait_ge(C.T.sem.h, C.T.sem.n)
                for q_ in sx4:
                    eng.wait_ge(q_.sem.h, q_.sem.n)
                eng.wait_ge(gcv.h, gcv.n)
                for gq in gcs2:
                    eng.wait_ge(gq.h, gq.n)
            pop_scope(nc)

            push_scope(nc)
            x1f = [sbuf2(nc, f"x1f{i}", [128, D], F32) for i in range(2)]
            x1w = sbuf2(nc, "x1w", [128, D], F32)
            xnb2 = sbuf2(nc, "xnb2", [128, D], BF16)
            xfm2 = sbuf2(nc, "xfm2", [128, SLAB * 1024], BF16)
            asb = sbuf2(nc, "asb", [128, 32 * 1024], BF16)
            g1f = sbuf2(nc, "g1f", [128, 32 * 1024], BF16)
            silt = sbuf2(nc, "silt", [128, 1024], BF16)
            w12b = [sbuf2(nc, f"w12b{i}", [128, 2048], BF16) for i in range(2)]
            hbuf = [sbuf2(nc, f"hb{i}", [128, 4096], BF16) for i in range(2)]
            x2t = [sbuf2(nc, f"x2t{i}", [128, D], F32) for i in range(2)]
            s1s = sbuf2(nc, "s1s", [128, 32], F32)
            s2s = sbuf2(nc, "s2s", [128, 32], F32)
            sh1s = sbuf2(nc, "sh1s", [128, 32], F32)
            sh2s = sbuf2(nc, "sh2s", [128, 32], F32)
            s3s = sbuf2(nc, "s3s", [128, D], F32)
            LD.start(s1s[:, :], s1r[:, :])
            LD.start(s2s[:, :], s2r[:, :])
            LD.start(sh1s[:, :], sh1r[:, :])
            LD.start(sh2s[:, :], sh2r[:, :])
            LD.start(s3s[:, :], s3rep[:, :])
            S.after(LD.sem)
            V.after(LD.sem)
            lw = [Dma(nc, nc.gpsimd, "lw0"), Dma(nc, nc.gpsimd, "lw1")]
            x2_last = {}
            x1_reload = [Dma(nc, nc.sync, "xr0"), Dma(nc, nc.sync, "xr1")]

            lnstate = {"x1used": {}}

            def ffn_ln_block(rt):
                prev_rd = C.T.sem.n if rt > 0 else None
                c_lastev = None
                for til in range(8):
                    ti = 8 * rt + til
                    buf = x1f[til % 2]
                    key = (rt, til - 2) if til >= 2 else (rt - 1, til + 6)
                    if key in lnstate["x1used"]:
                        lx[til % 2].after(C.V, lnstate["x1used"][key])
                    if til < 2 and rt > 0:
                        lx[til % 2].after(C.V)
                    c_ld = lx[til % 2].start(buf[:, :], x1out[128 * ti:128 * (ti + 1), :])
                    V.after(lx[til % 2].sem, c_ld)
                    S.after(lx[til % 2].sem, c_ld)
                    prev_rd, c_lastev = ln_transpose_tile(
                        C, nc, buf, xnb2,
                        lambda s, _t=til: xfm2[:, 1024 * s + 128 * _t:1024 * s + 128 * (_t + 1)],
                        L, sident, ptr, prev_rd)
                    lnstate["x1used"][(rt, til)] = C.V.sem.n
                lnstate["lastev"] = c_lastev

            for rt in range(4):
                ffn_ln_block(rt)
                t.wait_ge(C.S.sem.h, lnstate["lastev"])
                lw_done = {}
                evac_sil = {0: 0, 1: 0}
                evac_a = {0: 0, 1: 0}
                t.wait_ge(C.S.sem.h, C.S.sem.n)
                t.wait_ge(C.V.sem.h, C.V.sem.n)
                if rt == 0:
                    nc.gpsimd.wait_ge(gcs2[0].h, gcs2[0].n)
                    nc.gpsimd.wait_ge(gcs2[1].h, gcs2[1].n)
                for q_ in lw:
                    q_.e.wait_ge(q_.sem.h, q_.sem.n)
                    q_.after(C.T)
                for o in range(32):
                    wb = w12b[o % 2]
                    if o >= 2:
                        lw[o % 2].after(C.T, lw_done[o - 2])
                    lw[o % 2].start(wb[:, 0:1024], w1scr[o])
                    c_w = lw[o % 2].start(wb[:, 1024:2048], w2scr[o])
                    t.wait_ge(lw[o % 2].sem.h, c_w)
                    for th in range(2):
                        if evac_sil[th]:
                            t.wait_ge(C.S.sem.h, evac_sil[th])
                        for kc in range(8):
                            ins = t.matmul(pk[th][:, :], wb[:, 128 * kc:128 * (kc + 1)],
                                           xfm2[:, 1024 * kc + 512 * th:1024 * kc + 512 * (th + 1)],
                                           start=(kc == 0), stop=(kc == 7))
                        c1 = C.T.sem.inc(ins)
                        if evac_a[th]:
                            t.wait_ge(C.V.sem.h, evac_a[th])
                        for kc in range(8):
                            ins = t.matmul(pk[2 + th][:, :], wb[:, 1024 + 128 * kc:1024 + 128 * (kc + 1)],
                                           xfm2[:, 1024 * kc + 512 * th:1024 * kc + 512 * (th + 1)],
                                           start=(kc == 0), stop=(kc == 7))
                        c2 = C.T.sem.inc(ins)
                        S.after(C.T, c1)
                        if evac_a[th]:
                            S.after(C.V, evac_a[th])
                        evac_sil[th] = S.activation(silt[:, 512 * th:512 * (th + 1)], pk[th][:, :],
                                                    AF.Silu, scale=s1s[:, o:o + 1].opt())
                        V.after(C.S, evac_sil[th])
                        V.after(C.T, c2)
                        evac_a[th] = V.scalar_tensor_tensor(
                            asb[:, 1024 * o + 512 * th:1024 * o + 512 * (th + 1)],
                            pk[2 + th][:, :], s2s[:, o:o + 1].opt(),
                            silt[:, 512 * th:512 * (th + 1)], ALU.mult, ALU.mult)
                    lw_done[o] = C.T.sem.n
                for hscr, hs, srcb, dstb in ((h1scr, sh1s, asb, g1f), (h2scr, sh2s, g1f, asb)):
                    hb_done = {}
                    evac_g = {0: 0, 1: 0, 2: 0, 3: 0}
                    t.wait_ge(C.S.sem.h, C.S.sem.n)
                    t.wait_ge(C.V.sem.h, C.V.sem.n)
                    for q_ in lw:
                        q_.e.wait_ge(q_.sem.h, q_.sem.n)
                        q_.after(C.T)
                    for o in range(32):
                        hb = hbuf[o % 2]
                        if o >= 2:
                            lw[o % 2].after(C.T, hb_done[o - 2])
                        c_w = lw[o % 2].start(hb[:, :], hscr[o])
                        t.wait_ge(lw[o % 2].sem.h, c_w)
                        for th in range(2):
                            bank = 2 * (o % 2) + th
                            pp = pk[bank]
                            if evac_g[bank]:
                                t.wait_ge(C.S.sem.h, evac_g[bank])
                            for kc in range(32):
                                ins = t.matmul(pp[:, :], hb[:, 128 * kc:128 * (kc + 1)],
                                               srcb[:, 1024 * kc + 512 * th:1024 * kc + 512 * (th + 1)],
                                               start=(kc == 0), stop=(kc == 31))
                            c_mm = C.T.sem.inc(ins)
                            S.after(C.T, c_mm)
                            evac_g[bank] = S.activation(
                                dstb[:, 1024 * o + 512 * th:1024 * o + 512 * (th + 1)],
                                pp[:, :], AF.Gelu, scale=hs[:, o:o + 1].opt())
                        hb_done[o] = C.T.sem.n
                lw[0].after(C.T)
                lw[1].after(C.T)
                for q_ in lw:
                    q_.e.wait_ge(q_.sem.h, q_.sem.n)
                for kc in range(32):
                    lw[kc % 2].start(g1f[:, 1024 * kc:1024 * (kc + 1)], w3scr[kc])
                t.wait_ge(lw[0].sem.h, lw[0].sem.n)
                t.wait_ge(lw[1].sem.h, lw[1].sem.n)
                w3_resid = {0: 0, 1: 0, 2: 0, 3: 0}
                t.wait_ge(C.S.sem.h, C.S.sem.n)
                t.wait_ge(C.V.sem.h, C.V.sem.n)
                x1w_used = {}
                for til in range(8):
                    ti = 8 * rt + til
                    xb = x2t[til % 2]
                    xr = x1f[til % 2]
                    if til % 2 in x1w_used:
                        x1_reload[til % 2].after(C.V, x1w_used[til % 2])
                    else:
                        x1_reload[til % 2].after(C.V)
                    c_xr = x1_reload[til % 2].start(xr[:, :], x1out[128 * ti:128 * (ti + 1), :])
                    V.after(x1_reload[til % 2].sem, c_xr)
                    if til % 2 in x2_last:
                        V.after(sx[til % 2].sem, x2_last[til % 2])
                    c_r = None
                    for h in range(2):
                        bank = 2 * (til % 2) + h
                        pp = pk[bank]
                        if w3_resid[bank]:
                            t.wait_ge(C.V.sem.h, w3_resid[bank])
                        for kc in range(32):
                            ins = t.matmul(pp[:, :], asb[:, 1024 * kc + 128 * til:1024 * kc + 128 * (til + 1)],
                                           g1f[:, 1024 * kc + 512 * h:1024 * kc + 512 * (h + 1)],
                                           start=(kc == 0), stop=(kc == 31))
                        c_mm = C.T.sem.inc(ins)
                        V.after(C.T, c_mm)
                        V.tensor_tensor(xb[:, 512 * h:512 * (h + 1)], pp[:, :],
                                        s3s[:, 512 * h:512 * (h + 1)], ALU.mult)
                        c_r = V.tensor_tensor(xb[:, 512 * h:512 * (h + 1)],
                                              xb[:, 512 * h:512 * (h + 1)],
                                              xr[:, 512 * h:512 * (h + 1)], ALU.add)
                        w3_resid[bank] = c_r
                    x1w_used[til % 2] = c_r
                    sx[til % 2].after(C.V, c_r)
                    x2_last[til % 2] = sx[til % 2].start(
                        x2out[128 * ti:128 * (ti + 1), :], xb[:, :])
            sx[0].wait_all()
            sx[1].wait_all()
            pop_scope(nc)
    return nc


def prep_a_inputs(inp):
    x0 = np.asarray(inp["L0_attn_x"])
    ident = _bf16(np.eye(128, dtype=np.float32))
    mT = np.zeros((128, 4, 512), dtype=np.float32)
    for kt in range(4):
        kk = 128 * kt + np.arange(128)[:, None]
        mT[:, kt, :] = (kk <= np.arange(512)[None, :])
    maskT = _bf16(np.ascontiguousarray(mT.reshape(128, 2048)))
    onesb = _bf16(np.ones((128, 1), dtype=np.float32))
    onesf = np.ones((1, 128), dtype=np.float32)
    inv = 1.0 / (10000.0 ** (np.arange(0, 64, 2, dtype=np.float64) / 64.0))
    pos = np.arange(NTOK, dtype=np.float64)
    r = np.arange(128)
    fidx = (r % 64) % 32
    ang = pos[None, :] * inv[fidx][:, None]
    cosd = np.cos(ang).astype(np.float32)
    sind = np.sin(ang).astype(np.float32)
    sgn = np.where((r % 64) < 32, -1.0, 1.0).astype(np.float32)
    nsind = (sind * sgn[:, None]).astype(np.float32)

    def slabify(vec, nsl):
        return np.ascontiguousarray(np.asarray(vec).reshape(nsl, 128).T.astype(np.float32))

    lnw = slabify(inp["L0_attn_ln_w"], SLAB)
    lnb = slabify(inp["L0_attn_ln_b"], SLAB)
    s1r = slabify(inp["L0_ffn_w1_s"], 32)
    s2r = slabify(inp["L0_ffn_w2_s"], 32)
    sh1r = slabify(inp["L0_ffn_h1_s"], 32)
    sh2r = slabify(inp["L0_ffn_h2_s"], 32)
    s3rep = np.ascontiguousarray(np.broadcast_to(
        np.asarray(inp["L0_ffn_w3_s"])[None, :], (128, D))).astype(np.float32)
    common = dict(
        Wq=np.ascontiguousarray(np.asarray(inp["L0_attn_Wq"])),
        Wk=np.ascontiguousarray(np.asarray(inp["L0_attn_Wk"])),
        Wv=np.ascontiguousarray(np.asarray(inp["L0_attn_Wv"])),
        Wo=np.ascontiguousarray(np.asarray(inp["L0_attn_Wo"])),
        Wg1=np.ascontiguousarray(np.asarray(inp["L0_attn_Wg1"])),
        Wg2=np.ascontiguousarray(np.asarray(inp["L0_attn_Wg2"])),
        lnw=lnw, lnb=lnb, cosd=cosd, nsind=nsind,
        w1T=np.ascontiguousarray(np.asarray(inp["L0_ffn_w1_w"]).T),
        w2T=np.ascontiguousarray(np.asarray(inp["L0_ffn_w2_w"]).T),
        h1T=np.ascontiguousarray(np.asarray(inp["L0_ffn_h1_w"]).T),
        h2T=np.ascontiguousarray(np.asarray(inp["L0_ffn_h2_w"]).T),
        w3T=np.ascontiguousarray(np.asarray(inp["L0_ffn_w3_w"]).T),
        s1r=s1r, s2r=s2r, sh1r=sh1r, sh2r=sh2r, s3rep=s3rep,
        ident=ident, maskT=maskT, onesb=onesb, onesf=onesf,
    )
    maps = []
    for c in range(8):
        m = dict(common)
        m["x0"] = np.ascontiguousarray(x0[c].astype(np.float32))
        maps.append(m)
    return maps




_NC_CACHE = {}


def _get_nc(name, builder):
    if name not in _NC_CACHE:
        _NC_CACHE[name] = builder()
    return _NC_CACHE[name]


def kernel(**inputs):
    cores = list(range(8))

    nc_a = _get_nc("a", build_launch_a)
    res_a = run_bass_kernel_spmd(nc_a, prep_a_inputs(inputs), core_ids=cores)
    x2_full = np.stack([res_a.results[c]["x2"].reshape(NTOK, D) for c in range(8)])

    nc_b = _get_nc("b", build_launch_b)
    res_b = run_bass_kernel_spmd(
        nc_b, prep_b_inputs(x2_full, inputs["L1_attn_W_k"], inputs["L1_attn_W_v"],
                            inputs["L1_attn_W_q"]), core_ids=cores)
    x3 = np.empty((8, 4096, 1024), np.float32)
    amaxes = []
    for c in range(8):
        x3[:, 512 * c:512 * (c + 1), :] = res_b.results[c]["x3"].reshape(8, 512, 1024)
        amaxes.append(res_b.results[c]["amax1"].reshape(1))

    nc_c = _get_nc("c", build_launch_c)
    res_c = run_bass_kernel_spmd(
        nc_c, prep_c_inputs(x3, np.concatenate(amaxes), inputs["L1_ffn_weight"],
                            inputs["L1_ffn_bias"]), core_ids=cores)
    x4 = np.empty((8, 4096, 1024), np.float32)
    for c in range(8):
        x4[:, 512 * c:512 * (c + 1), :] = res_c.results[c]["x4"].reshape(8, 512, 1024)
    return x4


# version 07
# speedup vs baseline: 1.2571x; 1.2571x over previous
import numpy as np
from contextlib import ExitStack

import concourse.bass as bass
import concourse.mybir as mybir
from concourse.bass_utils import run_bass_kernel_spmd

F32 = mybir.dt.float32
BF16 = mybir.dt.bfloat16
F8 = mybir.dt.float8e4
AF = mybir.ActivationFunctionType
ALU = mybir.AluOpType
AX = mybir.AxisListType

D = 1024
SLAB = 8
NTOK = 4096
NTILE = 32
EPS = 1e-5


class Sem:

    def __init__(self, nc, name):
        self.h = nc._ctx.enter_context(nc.semaphore(name))
        self.n = 0

    def inc(self, instr, k=1):
        instr.then_inc(self.h, k)
        self.n += k
        return self.n


class Eng:

    def __init__(self, nc, eng, name, auto=True):
        self.e = eng
        self.auto = auto
        self.sem = Sem(nc, f"es_{name}")
        self._started = False

    def __getattr__(self, name):
        def emit(*a, **kw):
            if self.auto and self._started:
                self.e.wait_ge(self.sem.h, self.sem.n)
            self._started = True
            ins = getattr(self.e, name)(*a, **kw)
            return self.sem.inc(ins)
        return emit

    def after(self, other, val=None):
        sem = other.sem if isinstance(other, Eng) else other
        self.e.wait_ge(sem.h, sem.n if val is None else val)

    def mark(self):
        return self.sem.n


class Dma:

    def __init__(self, nc, eng, name):
        self.e = eng
        self.sem = Sem(nc, f"dq_{name}")

    def start(self, out, in_, **kw):
        return self.sem.inc(self.e.dma_start(out=out, in_=in_, **kw), 16)

    def after(self, other, val=None):
        sem = other.sem if isinstance(other, Eng) else other
        self.e.wait_ge(sem.h, sem.n if val is None else val)

    def wait_all(self):
        self.e.wait_ge(self.sem.h, self.sem.n)


class Core:

    def __init__(self, nc):
        self.nc = nc
        self.V = Eng(nc, nc.vector, "v")
        self.S = Eng(nc, nc.scalar, "s")
        self.G = Eng(nc, nc.gpsimd, "g")
        self.T = Eng(nc, nc.tensor, "t", auto=False)
        self.LD = Dma(nc, nc.sync, "ld")

    def dma(self, name, eng=None):
        return Dma(self.nc, eng or self.nc.sync, name)

    def tseq(self, *fns):
        pass


def new_nc():
    nc = bass.Bass("TRN2", target_bir_lowering=False)
    nc._ctx = ExitStack()
    return nc


def sbuf(nc, name, shape, dtype):
    return nc._ctx.enter_context(nc.sbuf_tensor(name, shape, dtype))


def psum(nc, name, shape, dtype=F32):
    return nc._ctx.enter_context(nc.psum_tensor(name, shape, dtype))



RSQRT_MAGIC = 1597463007


def rsqrt_v(C, out, x, L):
    V = C.V
    n = x.shape[-1]
    xi = x.bitcast(mybir.dt.int32)
    yi = L.n1.bitcast(mybir.dt.int32)[:, 0:n]
    V.tensor_scalar(yi, xi, 1, None, ALU.arith_shift_right)
    V.tensor_tensor(yi, L.cmagic[:, 0:n], yi, ALU.subtract)
    y = L.n1[:, 0:n]
    xh = L.n2[:, 0:n]
    V.tensor_scalar(xh, x, 0.5, None, ALU.mult)
    for _ in range(2):
        V.tensor_tensor(out, y, y, ALU.mult)
        V.tensor_tensor(out, out, xh, ALU.mult)
        V.tensor_scalar(out, out, -1.0, 1.5, ALU.mult, ALU.add)
        y_new = out
        c = V.tensor_tensor(out, y, out, ALU.mult)
        if _ == 0:
            V.tensor_copy(y, out)
    return c

class LnScratch:
    _magic = {}

    def __init__(self, nc, tag="", junk=None):
        self.junk = junk if junk is not None else sbuf(nc, f"lnjunk{tag}", [128, D], BF16)
        self.mean = sbuf(nc, f"lnmean{tag}", [128, 1], F32)
        self.rstd = sbuf(nc, f"lnrstd{tag}", [128, 1], F32)
        self.s_ = sbuf(nc, f"lns{tag}", [128, 1], F32)
        self.ss = sbuf(nc, f"lnss{tag}", [128, 1], F32)
        self.n1 = sbuf(nc, f"lnn1{tag}", [128, NB], F32)
        self.n2 = sbuf(nc, f"lnn2{tag}", [128, NB], F32)
        if id(nc) not in LnScratch._magic:
            LnScratch._magic[id(nc)] = sbuf(nc, f"cmagic{tag}", [128, NB], mybir.dt.int32)
        self.cmagic = LnScratch._magic[id(nc)][:, :]
        self._magic_init = False

    def init_magic(self, C):
        if not self._magic_init:
            C.V.memset(self.cmagic, RSQRT_MAGIC)
            LnScratch.__dict__
        self._magic_init = True


def ln_stats(C, x_ap, L):
    V, S = C.V, C.S
    L.init_magic(C)
    V.tensor_reduce(L.s_[:, :], x_ap, AX.X, ALU.add)
    if getattr(L, "_ss_reader", 0):
        S.after(C.V, L._ss_reader)
    c_ss = S.activation(L.junk[:, :], x_ap, AF.Square, accum_out=L.ss[:, :])
    c_mean = V.tensor_scalar(L.mean[:, :], L.s_[:, :], 1.0 / D, None, ALU.mult)
    V.after(C.S, c_ss)
    V.tensor_scalar(L.ss[:, :], L.ss[:, :], 1.0 / D, None, ALU.mult)
    L._ss_reader = V.tensor_scalar(L.s_[:, :], L.mean[:, :], L.mean[:, :], L.ss[:, :].opt(),
                                   ALU.mult, ALU.subtract)
    V.tensor_scalar(L.s_[:, :], L.s_[:, :], -1.0, EPS, ALU.mult, ALU.add)
    c_rstd = rsqrt_v(C, L.rstd[:, :], L.s_[:, :], L)
    return c_mean, c_rstd


def build_launch_c():
    nc = new_nc()
    x3 = nc.dram_tensor("x3", [NTOK, D], F32, kind="ExternalInput").ap()
    amax8 = nc.dram_tensor("amax8", [1, 8], F32, kind="ExternalInput").ap()
    wT = nc.dram_tensor("wT", [D, D], F32, kind="ExternalInput").ap()
    bias_rep = nc.dram_tensor("bias_rep", [128, D], F32, kind="ExternalInput").ap()
    ident = nc.dram_tensor("ident", [128, 128], BF16, kind="ExternalInput").ap()
    ones_f32 = nc.dram_tensor("ones_f32", [1, 128], F32, kind="ExternalInput").ap()
    x4 = nc.dram_tensor("x4", [NTOK, D], F32, kind="ExternalOutput").ap()
    wb = nc.dram_tensor("wb", [128], F32)

    sident = sbuf(nc, "sident", [128, 128], BF16)
    sw = [sbuf(nc, f"sw{k}", [128, D], F32) for k in range(8)]
    qw = [sbuf(nc, f"qw{k}", [128, D], BF16) for k in range(8)]
    qw8 = sbuf(nc, "qw8", [128, D], F8)
    sbias = sbuf(nc, "sbias", [128, D], F32)
    sones = sbuf(nc, "sones", [1, 128], F32)
    samax = sbuf(nc, "samax", [1, 8], F32)
    swcol = sbuf(nc, "swcol", [128, 8], F32)
    swrow = sbuf(nc, "swrow", [1, 128], F32)
    sscl = sbuf(nc, "sscl", [1, 8], F32)
    bxh = sbuf(nc, "bxh", [128, 1], F32)
    bwh = sbuf(nc, "bwh", [128, 1], F32)
    bosc = sbuf(nc, "bosc", [128, 1], F32)
    xt = [sbuf(nc, f"xt{i}", [128, D], F32) for i in range(2)]
    xn = [sbuf(nc, f"xn{i}", [128, D], F32) for i in range(2)]
    q8 = [sbuf(nc, f"q8{i}", [128, D], F8) for i in range(2)]
    qb = [sbuf(nc, f"qb{i}", [128, D], BF16) for i in range(2)]
    qfm = [sbuf(nc, f"qfm{i}", [128, SLAB * 128], BF16) for i in range(2)]
    outt = [sbuf(nc, f"outt{i}", [128, D], F32) for i in range(2)]
    Ls = [LnScratch(nc, "0"), LnScratch(nc, "1")]

    pbc = psum(nc, "pbc", [128, 1], F32)
    ptr = [psum(nc, f"ptr{i}", [128, 128], BF16) for i in range(2)]
    po = [psum(nc, f"po{i}", [128, 512], F32) for i in range(2)]

    with nc.Block() as block:
        @block.sync
        def _(sync):
            C = Core(nc)
            V, S, T, LD = C.V, C.S, C.T, C.LD
            ldm = C.dma("ldm")
            stw = C.dma("stw")
            ldx = [C.dma("ldx0"), C.dma("ldx1")]
            stx = [C.dma("stx0"), C.dma("stx1")]
            t = nc.tensor

            LD.start(sident[:, :], ident[:, :])
            LD.start(sbias[:, :], bias_rep[:, :])
            LD.start(sones[:, :], ones_f32[:, :])
            LD.start(samax[:, :], amax8[:, :])
            for k in range(8):
                LD.start(sw[k][:, :], wT[128 * k:128 * (k + 1), :])
            V.after(LD.sem)
            S.after(LD.sem)

            for k in range(8):
                V.tensor_reduce(swcol[:, k:k + 1], sw[k][:, :], AX.X, ALU.max,
                                apply_absolute_value=True)
            c_wred = V.tensor_reduce(swcol[:, 0:1], swcol[:, :], AX.X, ALU.max)
            stw.after(C.V, c_wred)
            c_wb = stw.start(wb[:], swcol[:, 0:1])
            ldm.e.wait_ge(stw.sem.h, c_wb)
            c_row = ldm.start(swrow[:, :], wb[None, :])
            V.after(ldm.sem, c_row)
            V.tensor_reduce(sscl[:, 0:1], swrow[:, :], AX.X, ALU.max,
                            apply_absolute_value=True)
            V.tensor_reduce(sscl[:, 1:2], samax[:, :], AX.X, ALU.max)
            V.reciprocal(sscl[:, 6:7], sscl[:, 0:1])
            V.reciprocal(sscl[:, 7:8], sscl[:, 1:2])
            V.tensor_scalar(sscl[:, 2:3], sscl[:, 6:7], 224.0, None, ALU.mult)
            V.tensor_scalar(sscl[:, 3:4], sscl[:, 7:8], 224.0, None, ALU.mult)
            V.tensor_scalar(sscl[:, 4:5], sscl[:, 2:3], sscl[:, 3:4], None, ALU.mult)
            c_osc = V.reciprocal(sscl[:, 5:6], sscl[:, 4:5])
            for j, (srcv, dst) in enumerate(((sscl[:, 2:3], bwh), (sscl[:, 3:4], bxh),
                                             (sscl[:, 5:6], bosc))):
                t.wait_ge(C.V.sem.h, C.V.sem.n)
                c_mm = T.matmul(pbc[:, :], sones[:, :], srcv, start=True, stop=True)
                V.after(C.T, c_mm)
                V.tensor_copy(dst[:, :], pbc[:, :])
            c_bcast = V.mark()
            S.after(C.V, c_bcast)

            for k in range(8):
                if k:
                    S.after(C.V)
                S.activation(qw8[:, :], sw[k][:, :], AF.Copy, scale=bwh[:, :].opt())
                V.after(C.S)
                V.tensor_copy(qw[k][:, :], qw8[:, :])
            c_qw = V.mark()
            t.wait_ge(C.V.sem.h, c_qw)

            resid_done = {}
            store_done = {}
            quantD = {}
            transpD = {}
            epiD = {}
            evac_done = {}

            def stage_A(i):
                p = i % 2
                buf = xt[p]
                lq = ldx[p]
                if i >= 2:
                    lq.after(C.V, resid_done[i - 2])
                c_ld = lq.start(buf[:, :], x3[128 * i:128 * (i + 1), :])
                V.after(lq.sem, c_ld)
                S.after(lq.sem, c_ld)
                c_mean, c_rstd = ln_stats(C, buf[:, :], Ls[p])
                if i >= 2:
                    V.after(C.S, quantD[i - 2])
                c_xn = V.tensor_scalar(xn[p][:, :], buf[:, :], Ls[p].mean[:, :],
                                       Ls[p].rstd[:, :], ALU.subtract, ALU.mult)
                S.after(C.V, c_xn)
                quantD[i] = S.activation(q8[p][:, :], xn[p][:, :], AF.Copy,
                                         scale=bxh[:, :].opt())
                V.after(C.S, quantD[i])
                if i >= 2:
                    V.after(C.T, transpD[i - 2])
                return V.tensor_copy(qb[p][:, :], q8[p][:, :])

            def stage_B(i, c_qb):
                p = i % 2
                t.wait_ge(C.V.sem.h, c_qb)
                for s in range(SLAB):
                    pt = ptr[s % 2]
                    if (i, s - 2) in evac_done:
                        t.wait_ge(C.S.sem.h, evac_done[(i, s - 2)])
                    elif i >= 1 and s < 2:
                        t.wait_ge(C.S.sem.h, evac_done[(i - 1, s + SLAB - 2)])
                    c_tr = T.transpose(pt[:, :], qb[p][:, 128 * s:128 * (s + 1)], sident[:, :])
                    S.after(C.T, c_tr)
                    evac_done[(i, s)] = S.copy(qfm[p][:, 128 * s:128 * (s + 1)], pt[:, :])
                transpD[i] = C.T.sem.n

            def stage_C(i):
                p = i % 2
                buf = xt[p]
                t.wait_ge(C.S.sem.h, evac_done[(i, SLAB - 1)])
                if i >= 1:
                    t.wait_ge(C.S.sem.h, epiD[i - 1])
                ins = None
                for h in range(2):
                    for k in range(8):
                        ins = nc.tensor.matmul(po[h][:, :], qfm[p][:, 128 * k:128 * (k + 1)],
                                               qw[k][:, 512 * h:512 * (h + 1)],
                                               start=(k == 0), stop=(k == 7))
                c_mm2 = C.T.sem.inc(ins)
                S.after(C.T, c_mm2)
                ot = outt[p]
                if i >= 2:
                    S.after(stx[p].sem, store_done[i - 2])
                for h in range(2):
                    S.activation(ot[:, 512 * h:512 * (h + 1)], po[h][:, :],
                                 AF.Copy, scale=bosc[:, :].opt())
                epiD[i] = S.mark()
                V.after(C.S, epiD[i])
                V.tensor_tensor(ot[:, :], ot[:, :], sbias[:, :], ALU.add)
                resid_done[i] = V.tensor_tensor(ot[:, :], ot[:, :], buf[:, :], ALU.add)
                stx[p].after(C.V, resid_done[i])
                store_done[i] = stx[p].start(x4[128 * i:128 * (i + 1), :], ot[:, :])

            c_qb0 = stage_A(0)
            stage_B(0, c_qb0)
            for i in range(NTILE):
                if i + 1 < NTILE:
                    c_qbn = stage_A(i + 1)
                stage_C(i)
                if i + 1 < NTILE:
                    stage_B(i + 1, c_qbn)
            stx[0].wait_all()
            stx[1].wait_all()
    return nc



def _bf16(x):
    import ml_dtypes
    return x.astype(ml_dtypes.bfloat16)


def prep_c_inputs(x3_full, amax_partials, weight, bias):
    wT = np.ascontiguousarray(np.asarray(weight).T)
    bias_rep = np.ascontiguousarray(np.broadcast_to(np.asarray(bias)[None, :], (128, D)))
    ident = _bf16(np.eye(128, dtype=np.float32))
    ones = np.ones((1, 128), dtype=np.float32)
    amax8 = np.asarray(amax_partials, dtype=np.float32).reshape(1, 8)
    maps = []
    for c in range(8):
        x3c = np.ascontiguousarray(
            np.asarray(x3_full)[:, 512 * c:512 * (c + 1), :].reshape(NTOK, D))
        maps.append(dict(x3=x3c, amax8=amax8, wT=wT, bias_rep=bias_rep,
                         ident=ident, ones_f32=ones))
    return maps


NB = 128
BLK = 8



def g_reduce8(G, dst3, src4):
    G.tensor_tensor(src4[:, :, :, 0:4] if src4.ndim == 4 else src4[:, :, 0:4],
                    src4[:, :, :, 0:4] if src4.ndim == 4 else src4[:, :, 0:4],
                    src4[:, :, :, 4:8] if src4.ndim == 4 else src4[:, :, 4:8], ALU.add)
    G.tensor_tensor(src4[:, :, :, 0:2] if src4.ndim == 4 else src4[:, :, 0:2],
                    src4[:, :, :, 0:2] if src4.ndim == 4 else src4[:, :, 0:2],
                    src4[:, :, :, 2:4] if src4.ndim == 4 else src4[:, :, 2:4], ALU.add)
    return G.tensor_tensor(dst3,
                           src4[:, :, :, 0:1] if src4.ndim == 4 else src4[:, :, 0:1],
                           src4[:, :, :, 1:2] if src4.ndim == 4 else src4[:, :, 1:2], ALU.add)


def build_launch_b():
    nc = new_nc()
    x2 = nc.dram_tensor("x2", [NTOK, D], F32, kind="ExternalInput").ap()
    wkT = nc.dram_tensor("wkT", [D, D], F32, kind="ExternalInput").ap()
    wvT = nc.dram_tensor("wvT", [D, D], F32, kind="ExternalInput").ap()
    wqT = nc.dram_tensor("wqT", [D, D], F32, kind="ExternalInput").ap()
    ident = nc.dram_tensor("ident", [128, 128], BF16, kind="ExternalInput").ap()
    x3 = nc.dram_tensor("x3", [NTOK, D], F32, kind="ExternalOutput").ap()
    amax1 = nc.dram_tensor("amax1", [1, 1], F32, kind="ExternalOutput").ap()
    gb = nc.dram_tensor("gb", [128], F32)

    sident = sbuf(nc, "sident", [128, 128], BF16)
    wk = [sbuf(nc, f"wk{k}", [128, D], BF16) for k in range(8)]
    wv = [sbuf(nc, f"wv{k}", [128, D], BF16) for k in range(8)]
    wq = [sbuf(nc, f"wq{k}", [128, D], BF16) for k in range(8)]
    xt = [sbuf(nc, f"xt{i}", [128, D], F32) for i in range(2)]
    xr = [sbuf(nc, f"xr{i}", [128, D], F32) for i in range(2)]
    xnb1 = sbuf(nc, "xnb1", [128, D], BF16)
    xnb = [xnb1, xnb1]
    xfm = sbuf(nc, "xfm", [128, SLAB * 128], BF16)
    kvqall = sbuf(nc, "kvqall", [128, 8 * 3 * D], BF16)
    nrm8 = sbuf(nc, "nrm8", [128, 8 * NB], F32)
    rkn8 = sbuf(nc, "rkn8", [128, 8 * NB], F32)
    S8 = sbuf(nc, "S8", [128, NB * BLK * BLK], BF16)
    tmp8 = sbuf(nc, "tmp8", [128, NB * BLK * BLK], BF16)
    kn = sbuf(nc, "kn", [128, D], BF16)
    retr = sbuf(nc, "retr", [128, D], F32)
    delta = sbuf(nc, "delta", [128, D], BF16)
    sq_ = nrm8
    sil = sbuf(nc, "sil", [128, D], F32)
    sc8 = sil
    tmax = sbuf(nc, "tmax", [128, 1], F32)
    gmax = sbuf(nc, "gmax", [128, 1], F32)
    grow = sbuf(nc, "grow", [1, 128], F32)
    ga = sbuf(nc, "ga", [1, 1], F32)
    L = LnScratch(nc, junk=xfm)

    ptr = [psum(nc, f"ptr{i}", [128, 128], BF16) for i in range(2)]
    pk = [psum(nc, f"pk{i}", [128, 512], F32) for i in range(4)]

    FV = 4864
    NBV = FV // 64
    JV = NBV * BLK

    def s4v(ap):
        return ap[:, 0:FV].rearrange("p (nb i j) -> p nb i j", nb=NBV, i=BLK, j=BLK)

    def s4g(ap):
        return ap[:, FV:].rearrange("p (nb i j) -> p nb i j", nb=NB - NBV, i=BLK, j=BLK)

    def v3v(ap):
        return ap[:, 0:JV].rearrange("p (nb j) -> p nb j", nb=NBV, j=BLK)

    def v3g(ap):
        return ap[:, JV:].rearrange("p (nb j) -> p nb j", nb=NB - NBV, j=BLK)

    with nc.Block() as block:
        @block.sync
        def _(sync):
            C = Core(nc)
            V, S, G, T, LD = C.V, C.S, C.G, C.T, C.LD
            GD = Dma(nc, nc.gpsimd, "gld")
            lx = [C.dma("lx0"), C.dma("lx1")]
            lr = [C.dma("lr0"), C.dma("lr1")]
            sx = [C.dma("sx0"), C.dma("sx1")]
            stw = C.dma("stw")
            ldm = C.dma("ldm")
            t = nc.tensor

            LD.start(sident[:, :], ident[:, :])
            for k in range(8):
                GD.start(wk[k][:, :], wkT[128 * k:128 * (k + 1), :])
                GD.start(wv[k][:, :], wvT[128 * k:128 * (k + 1), :])
                GD.start(wq[k][:, :], wqT[128 * k:128 * (k + 1), :])
            V.after(LD.sem)
            c_init = V.memset(gmax[:, :], 0.0)

            last_store = {}
            last_xload = {}
            last_rload = {}
            tail_jobs = []
            amax_s = {}

            def emit_tail(tstep, par, ti, qt, c_tanh):
                if par in last_rload:
                    lr[par].after(C.V, last_rload[par])
                if par in amax_s:
                    lr[par].after(C.S, amax_s[par])
                if par in last_store:
                    lr[par].after(sx[par].sem, last_store[par])
                c_xr = lr[par].start(xr[par][:, :], x2[128 * ti:128 * (ti + 1), :])
                q_bv = v3v(qt).unsqueeze(2).broadcast_to([128, NBV, BLK, BLK])
                q_bg = v3g(qt).unsqueeze(2).broadcast_to([128, NB - NBV, BLK, BLK])
                V.after(C.S, c_tanh)
                G.after(C.S, c_tanh)
                V.tensor_tensor(s4v(tmp8), s4v(S8), q_bv, ALU.mult)
                c_sv = V.tensor_reduce(v3v(sq_), s4v(tmp8), AX.X, ALU.add)
                G.tensor_tensor(s4g(tmp8), s4g(S8), q_bg, ALU.mult)
                c_sg = g_reduce8(G, v3g(sq_).unsqueeze(3), s4g(tmp8))
                S.after(C.V, c_sv)
                S.after(C.G, c_sg)
                c_sil = S.activation(sil[:, :], sq_[:, :], AF.Silu)
                G.after(C.S, c_sil)
                c_om = G.tensor_tensor(sil[:, :], sq_[:, :], sil[:, :], ALU.mult)
                if par in last_store:
                    V.after(sx[par].sem, last_store[par])
                V.after(lr[par].sem, c_xr)
                V.after(C.G, c_om)
                c_x3 = V.tensor_tensor(xr[par][:, :], xr[par][:, :], sil[:, :], ALU.add)
                last_rload[par] = c_x3
                S.after(C.V, c_x3)
                c_m2, c_r2 = ln_stats(C, xr[par][:, :], L)
                V.tensor_scalar(sc8[:, :], xr[par][:, :], L.mean[:, :], L.rstd[:, :],
                                ALU.subtract, ALU.mult)
                V.tensor_reduce(tmax[:, :], sc8[:, :], AX.X, ALU.max,
                                apply_absolute_value=True)
                c_gm = V.tensor_tensor(gmax[:, :], gmax[:, :], tmax[:, :], ALU.max)
                sx[par].after(C.V, c_x3)
                last_store[par] = sx[par].start(x3[128 * ti:128 * (ti + 1), :], xr[par][:, :])
                last_rload[par] = c_gm
                amax_s[par] = C.S.sem.n

            for rt in range(4):
                while tail_jobs:
                    emit_tail(*tail_jobs.pop(0))
                V.after(C.S)
                V.after(C.G)
                c_zS = V.memset(S8[:, :], 0.0)
                G.after(C.V, c_zS)

                proj_evac = {}
                prev_tr = None
                for tstep in range(8):
                    ti = tstep * 4 + rt
                    par = tstep % 2
                    if par in last_xload:
                        lx[par].after(C.V, last_xload[par])
                    c_ld = lx[par].start(xt[par][:, :], x2[128 * ti:128 * (ti + 1), :])
                    V.after(lx[par].sem, c_ld)
                    S.after(lx[par].sem, c_ld)
                    c_mean, c_rstd = ln_stats(C, xt[par][:, :], L)
                    if prev_tr is not None:
                        V.after(C.T, prev_tr)
                    c_xnb = V.tensor_scalar(xnb[par][:, :], xt[par][:, :], L.mean[:, :],
                                            L.rstd[:, :], ALU.subtract, ALU.mult)
                    last_xload[par] = c_xnb
                    t.wait_ge(C.V.sem.h, c_xnb)
                    ev = {}
                    for s in range(SLAB):
                        pt = ptr[s % 2]
                        if s >= 2:
                            t.wait_ge(C.S.sem.h, ev[s - 2])
                        c_tr = T.transpose(pt[:, :], xnb[par][:, 128 * s:128 * (s + 1)],
                                           sident[:, :])
                        S.after(C.T, c_tr)
                        ev[s] = S.copy(xfm[:, 128 * s:128 * (s + 1)], pt[:, :])
                    prev_tr = c_tr
                    t.wait_ge(C.S.sem.h, ev[SLAB - 1])
                    if rt == 0 and tstep == 0:
                        t.wait_ge(GD.sem.h, GD.sem.n)
                    for wi, wmat in enumerate((wk, wv, wq)):
                        for h in range(2):
                            gidx = wi * 2 + h
                            pp = pk[gidx % 4]
                            key = (tstep, gidx)
                            prev = proj_evac.get((tstep, gidx - 4)) or \
                                proj_evac.get((tstep - 1, gidx + 2))
                            if prev:
                                t.wait_ge(C.S.sem.h, prev)
                            for kc in range(8):
                                ins = t.matmul(pp[:, :], xfm[:, 128 * kc:128 * (kc + 1)],
                                               wmat[kc][:, 512 * h:512 * (h + 1)],
                                               start=(kc == 0), stop=(kc == 7))
                            c_mm = C.T.sem.inc(ins)
                            S.after(C.T, c_mm)
                            proj_evac[key] = S.copy(
                                kvqall[:, 3072 * tstep + D * wi + 512 * h:
                                       3072 * tstep + D * wi + 512 * (h + 1)], pp[:, :])
                    kt = kvqall[:, 3072 * tstep:3072 * tstep + D]
                    V.after(C.S, proj_evac[(tstep, 1)])
                    V.tensor_tensor(retr[:, :], kt, kt, ALU.mult)
                    V.tensor_reduce(nrm8[:, NB * tstep:NB * (tstep + 1)],
                                    retr[:, :].rearrange("p (nb j) -> p nb j", nb=NB, j=BLK),
                                    AX.X, ALU.add)
                c_n8 = V.mark()
                S.after(C.V, c_n8)
                c_sq8 = S.activation(nrm8[:, :], nrm8[:, :], AF.Sqrt)
                V.after(C.S, c_sq8)
                V.tensor_scalar(nrm8[:, :], nrm8[:, :], 1e-6, None, ALU.add)
                c_rk8 = V.reciprocal(rkn8[:, :], nrm8[:, :])
                G.after(C.V, c_rk8)

                for tstep in range(8):
                    ti = tstep * 4 + rt
                    par = tstep % 2
                    kt = kvqall[:, 3072 * tstep:3072 * tstep + D]
                    vt = kvqall[:, 3072 * tstep + D:3072 * tstep + 2 * D]
                    qt = kvqall[:, 3072 * tstep + 2 * D:3072 * tstep + 3 * D]
                    rk = rkn8[:, NB * tstep:NB * (tstep + 1)]
                    if tail_jobs:
                        V.after(C.S, tail_jobs[-1][-1])
                        G.after(C.S, tail_jobs[-1][-1])
                    c_knv = V.tensor_tensor(
                        v3v(kn), v3v(kt),
                        rk[:, 0:NBV].unsqueeze(2).broadcast_to([128, NBV, BLK]), ALU.mult)
                    c_kng = G.tensor_tensor(
                        v3g(kn), v3g(kt),
                        rk[:, NBV:].unsqueeze(2).broadcast_to([128, NB - NBV, BLK]), ALU.mult)
                    kn_bv = v3v(kn).unsqueeze(2).broadcast_to([128, NBV, BLK, BLK])
                    kn_bg = v3g(kn).unsqueeze(2).broadcast_to([128, NB - NBV, BLK, BLK])
                    V.tensor_tensor(s4v(tmp8), s4v(S8), kn_bv, ALU.mult)
                    c_rv = V.tensor_reduce(v3v(retr), s4v(tmp8), AX.X, ALU.add)
                    G.tensor_tensor(s4g(tmp8), s4g(S8), kn_bg, ALU.mult)
                    c_rg = g_reduce8(G, v3g(retr).unsqueeze(3), s4g(tmp8))
                    c_dv = V.tensor_tensor(delta[:, 0:JV], vt[:, 0:JV], retr[:, 0:JV],
                                           ALU.subtract)
                    c_dg = G.tensor_tensor(delta[:, JV:], vt[:, JV:], retr[:, JV:],
                                           ALU.subtract)
                    d_bv = v3v(delta).unsqueeze(3).broadcast_to([128, NBV, BLK, BLK])
                    d_bg = v3g(delta).unsqueeze(3).broadcast_to([128, NB - NBV, BLK, BLK])
                    if tail_jobs:
                        emit_tail(*tail_jobs.pop(0))
                    V.tensor_tensor(s4v(tmp8), d_bv, kn_bv, ALU.mult)
                    c_av = V.tensor_tensor(S8[:, 0:FV], S8[:, 0:FV], tmp8[:, 0:FV], ALU.add)
                    G.tensor_tensor(s4g(tmp8), d_bg, kn_bg, ALU.mult)
                    c_ag = G.tensor_tensor(S8[:, FV:], S8[:, FV:], tmp8[:, FV:], ALU.add)
                    S.after(C.V, c_av)
                    S.after(C.G, c_ag)
                    c_tanh = S.activation(S8[:, :], S8[:, :], AF.Tanh)
                    tail_jobs.append((tstep, par, ti, qt, c_tanh))

            while tail_jobs:
                emit_tail(*tail_jobs.pop(0))
            stw.after(C.V)
            c_gb = stw.start(gb[:], gmax[:, :])
            ldm.e.wait_ge(stw.sem.h, c_gb)
            c_grow = ldm.start(grow[:, :], gb[None, :])
            V.after(ldm.sem, c_grow)
            c_ga = V.tensor_reduce(ga[:, :], grow[:, :], AX.X, ALU.max)
            stw.after(C.V, c_ga)
            stw.start(amax1[:, :], ga[:, :])
            sx[0].wait_all()
            sx[1].wait_all()
            stw.wait_all()
    return nc


def prep_b_inputs(x2_full, W_k, W_v, W_q):
    ident = _bf16(np.eye(128, dtype=np.float32))
    wkT = np.ascontiguousarray(np.asarray(W_k).T)
    wvT = np.ascontiguousarray(np.asarray(W_v).T)
    wqT = np.ascontiguousarray(np.asarray(W_q).T)
    maps = []
    for c in range(8):
        x2c = np.ascontiguousarray(
            np.asarray(x2_full)[:, 512 * c:512 * (c + 1), :].reshape(NTOK, D))
        maps.append(dict(x2=x2c, wkT=wkT, wvT=wvT, wqT=wqT, ident=ident))
    return maps



def push_scope(nc):
    s = ExitStack()
    nc._scopes.append(s)
    return s


def pop_scope(nc):
    nc._scopes.pop().close()


def sbuf2(nc, name, shape, dtype):
    st = nc._scopes[-1] if getattr(nc, "_scopes", None) else nc._ctx
    return st.enter_context(nc.sbuf_tensor(name, shape, dtype))


def ln_transpose_tile(C, nc, buf, xnb, xfm_dst_cols, L, sident, ptr, prev_xnb_reader):
    V, S, T = C.V, C.S, C.T
    t = nc.tensor
    c_mean, c_rstd = ln_stats(C, buf[:, :], L)
    if prev_xnb_reader is not None:
        V.after(C.T, prev_xnb_reader)
    c_xnb = V.tensor_scalar(xnb[:, :], buf[:, :], L.mean[:, :], L.rstd[:, :],
                            ALU.subtract, ALU.mult)
    t.wait_ge(C.V.sem.h, c_xnb)
    ev = {}
    c_tr = None
    for s in range(SLAB):
        pt = ptr[s % 2]
        if s >= 2:
            t.wait_ge(C.S.sem.h, ev[s - 2])
        c_tr = T.transpose(pt, xnb[:, 128 * s:128 * (s + 1)], sident[:, :])
        S.after(C.T, c_tr)
        ev[s] = S.copy(xfm_dst_cols(s), pt[:, :])
    return c_tr, ev[SLAB - 1]


def build_launch_a():
    nc = new_nc()
    nc._scopes = []
    x0 = nc.dram_tensor("x0", [NTOK, D], F32, kind="ExternalInput").ap()
    Wq = nc.dram_tensor("Wq", [D, D], F32, kind="ExternalInput").ap()
    Wk = nc.dram_tensor("Wk", [D, D], F32, kind="ExternalInput").ap()
    Wv = nc.dram_tensor("Wv", [D, D], F32, kind="ExternalInput").ap()
    Wo = nc.dram_tensor("Wo", [D, D], F32, kind="ExternalInput").ap()
    Wg1 = nc.dram_tensor("Wg1", [D, 64], F32, kind="ExternalInput").ap()
    Wg2 = nc.dram_tensor("Wg2", [64, D], F32, kind="ExternalInput").ap()
    lnw = nc.dram_tensor("lnw", [128, SLAB], F32, kind="ExternalInput").ap()
    lnb = nc.dram_tensor("lnb", [128, SLAB], F32, kind="ExternalInput").ap()
    cosd = nc.dram_tensor("cosd", [128, NTOK], F32, kind="ExternalInput").ap()
    nsind = nc.dram_tensor("nsind", [128, NTOK], F32, kind="ExternalInput").ap()
    w1T = nc.dram_tensor("w1T", [D, 4096], F32, kind="ExternalInput").ap()
    w2T = nc.dram_tensor("w2T", [D, 4096], F32, kind="ExternalInput").ap()
    h1T = nc.dram_tensor("h1T", [4096, 4096], F32, kind="ExternalInput").ap()
    h2T = nc.dram_tensor("h2T", [4096, 4096], F32, kind="ExternalInput").ap()
    w3T = nc.dram_tensor("w3T", [4096, D], F32, kind="ExternalInput").ap()
    s1r = nc.dram_tensor("s1r", [128, 32], F32, kind="ExternalInput").ap()
    s2r = nc.dram_tensor("s2r", [128, 32], F32, kind="ExternalInput").ap()
    sh1r = nc.dram_tensor("sh1r", [128, 32], F32, kind="ExternalInput").ap()
    sh2r = nc.dram_tensor("sh2r", [128, 32], F32, kind="ExternalInput").ap()
    s3rep = nc.dram_tensor("s3rep", [128, D], F32, kind="ExternalInput").ap()
    ident = nc.dram_tensor("ident", [128, 128], BF16, kind="ExternalInput").ap()
    maskT = nc.dram_tensor("maskT", [128, 2048], BF16, kind="ExternalInput").ap()
    onesb = nc.dram_tensor("onesb", [128, 1], BF16, kind="ExternalInput").ap()
    onesf = nc.dram_tensor("onesf", [1, 128], F32, kind="ExternalInput").ap()
    x1out = nc.dram_tensor("x1", [NTOK, D], F32, kind="ExternalOutput").ap()
    x2out = nc.dram_tensor("x2", [NTOK, D], F32, kind="ExternalOutput").ap()
    w1scr = nc.dram_tensor("w1scr", [32, 128, 1024], BF16).ap()
    w2scr = nc.dram_tensor("w2scr", [32, 128, 1024], BF16).ap()
    h1scr = nc.dram_tensor("h1scr", [32, 128, 4096], BF16).ap()
    h2scr = nc.dram_tensor("h2scr", [32, 128, 4096], BF16).ap()
    w3scr = nc.dram_tensor("w3scr", [32, 128, 1024], BF16).ap()

    sident = sbuf(nc, "sident", [128, 128], BF16)
    smask = sbuf(nc, "smask", [128, 2048], BF16)
    slnw = sbuf(nc, "slnw", [128, SLAB], F32)
    slnb = sbuf(nc, "slnb", [128, SLAB], F32)
    sonesb = sbuf(nc, "sonesb", [128, 1], BF16)
    sonesf = sbuf(nc, "sonesf", [1, 128], F32)
    L = LnScratch(nc)

    px = [psum(nc, f"px{i}", [128, 512], F32) for i in range(2)]
    ptr = [p[:].bitcast(BF16)[:, 0:128] for p in px]
    pk = [psum(nc, f"pk{i}", [128, 512], F32) for i in range(4)]
    pout = psum(nc, "pout", [128, 128], F32)
    pstate = psum(nc, "pstate", [128, 64], F32)

    with nc.Block() as block:
        @block.sync
        def _(sync):
            C = Core(nc)
            V, S, G, T, LD = C.V, C.S, C.G, C.T, C.LD
            GD = Dma(nc, nc.gpsimd, "gld")
            lx = [C.dma("lx0"), C.dma("lx1")]
            lx4 = [C.dma(f"lx4_{i}") for i in range(4)]
            sx4 = [C.dma(f"sx4_{i}") for i in range(4)]
            sx = [C.dma("sx0"), C.dma("sx1")]
            lrope = C.dma("lrope")
            t = nc.tensor

            LD.start(sident[:, :], ident[:, :])
            LD.start(smask[:, :], maskT[:, :])
            LD.start(slnw[:, :], lnw[:, :])
            LD.start(slnb[:, :], lnb[:, :])
            LD.start(sonesb[:, :], onesb[:, :])
            LD.start(sonesf[:, :], onesf[:, :])
            V.after(LD.sem)
            S.after(LD.sem)
            T.e.wait_ge(LD.sem.h, LD.sem.n)

            push_scope(nc)
            wq = [sbuf2(nc, f"wq{k}", [128, D], BF16) for k in range(8)]
            wk = [sbuf2(nc, f"wk{k}", [128, D], BF16) for k in range(8)]
            wv = [sbuf2(nc, f"wv{k}", [128, D], BF16) for k in range(8)]
            wo = [sbuf2(nc, f"wo{k}", [128, D], BF16) for k in range(8)]
            wg1 = sbuf2(nc, "wg1", [128, SLAB * 64], BF16)
            wg2 = sbuf2(nc, "wg2", [64, D], BF16)
            scos = [sbuf2(nc, f"scos{i}", [128, 512], F32) for i in range(2)]
            snsin = [sbuf2(nc, f"snsin{i}", [128, 512], F32) for i in range(2)]
            x0t = [sbuf2(nc, f"x0t{i}", [128, D], F32) for i in range(4)]
            xnb = sbuf2(nc, "xnb", [128, D], BF16)
            xfm = sbuf2(nc, "xfm", [128, SLAB * 512], BF16)
            qraw = [sbuf2(nc, f"qraw{i}", [128, 512], F32) for i in range(2)]
            rtmp = sbuf2(nc, "rtmp", [128, 512], F32)
            rcr = sbuf2(nc, "rcr", [128, 512], F32)
            Qr = sbuf2(nc, "Qr", [128, SLAB * 512], BF16)
            Kr = sbuf2(nc, "Kr", [128, SLAB * 512], BF16)
            ktm = sbuf2(nc, "ktm", [128, 4 * D], BF16)
            vtm = sbuf2(nc, "vtm", [128, 4 * D], BF16)
            Sf = sbuf2(nc, "Sf", [128, SLAB * 64], F32)
            Sb = sbuf2(nc, "Sb", [128, SLAB * 64], BF16)
            eA = sbuf2(nc, "eA", [128, 512], BF16)
            eB = sbuf2(nc, "eB", [128, 512], BF16)
            attn = sbuf2(nc, "attn", [128, SLAB * 512], BF16)
            gate = sbuf2(nc, "gate", [128, SLAB * 512], BF16)
            g1sb = sbuf2(nc, "g1sb", [64, 512], BF16)
            attq = sbuf2(nc, "attq", [128, SLAB * 512], BF16)
            srow = sbuf2(nc, "srow", [1, 512], F32)
            sqrow = sbuf2(nc, "sqrow", [1, 512], F32)
            murow = sbuf2(nc, "murow", [1, 512], F32)
            rsrow = sbuf2(nc, "rsrow", [1, 512], F32)
            mub = sbuf2(nc, "mub", [128, 512], F32)
            rsb = sbuf2(nc, "rsb", [128, 512], F32)
            nrmd = sbuf2(nc, "nrmd", [128, SLAB * 512], BF16)

            for k in range(8):
                GD.start(wq[k][:, :], Wq[128 * k:128 * (k + 1), :])
                GD.start(wk[k][:, :], Wk[128 * k:128 * (k + 1), :])
                GD.start(wv[k][:, :], Wv[128 * k:128 * (k + 1), :])
                GD.start(wo[k][:, :], Wo[128 * k:128 * (k + 1), :])
                GD.start(wg1[:, 64 * k:64 * (k + 1)], Wg1[128 * k:128 * (k + 1), :])
            GD.start(wg2[:, :], Wg2[:, :])
            V.memset(Sf[:, :], 0.0)
            V.memset(Sb[:, :], 0.0)

            wstage = [sbuf2(nc, f"wst{i}", [128, 2048], BF16) for i in range(2)]
            gcv = Sem(nc, "gcv")
            gcs2 = [Sem(nc, "gcs0"), Sem(nc, "gcs1")]
            conv_jobs = []
            for o in range(32):
                conv_jobs.append((w1T[:, 128 * o:128 * (o + 1)]
                                  .rearrange("(kc p) c -> p kc c", p=128),
                                  w1scr[o], 1024))
            for o in range(32):
                conv_jobs.append((w2T[:, 128 * o:128 * (o + 1)]
                                  .rearrange("(kc p) c -> p kc c", p=128),
                                  w2scr[o], 1024))
            for o in range(32):
                conv_jobs.append((w3T[128 * o:128 * (o + 1), :], w3scr[o], 1024))
            for hT, hscr in ((h1T, h1scr), (h2T, h2scr)):
                for o in range(32):
                    for half in range(2):
                        conv_jobs.append((
                            hT[2048 * half:2048 * (half + 1), 128 * o:128 * (o + 1)]
                            .rearrange("(kc p) c -> p kc c", p=128),
                            hscr[o][:, 2048 * half:2048 * (half + 1)], 2048))
            g_ = nc.gpsimd
            store_hist = []
            for i, (srcap, dstap, width) in enumerate(conv_jobs):
                stg = wstage[i % 2]
                if i >= 2:
                    g_.wait_ge(gcs2[i % 2].h, store_hist[i - 2])
                ldap = stg[:, 0:width]
                if width == 1024:
                    ldap = stg[:, 0:1024]
                if srcap.ndim == 3:
                    ldap = ldap.rearrange("p (kc c) -> p kc c", kc=srcap.shape[1])
                c_l = gcv.inc(g_.dma_start(out=ldap, in_=srcap), 16)
                g_.wait_ge(gcv.h, c_l)
                store_hist.append(gcs2[i % 2].inc(g_.dma_start(out=dstap, in_=stg[:, 0:width]), 16))

            rope_used = {}
            x0_used = {}
            x1_last = {}
            prev_mask_last = 0
            attn_evac_last = 0
            state_prev_slab = 0
            for g in range(8):
                if g >= 2:
                    lrope.after(C.V, rope_used[g - 2])
                lrope.start(scos[g % 2][:, :], cosd[:, 512 * g:512 * (g + 1)])
                c_rt = lrope.start(snsin[g % 2][:, :], nsind[:, 512 * g:512 * (g + 1)])
                prev_rd = None
                for til in range(4):
                    ti = 4 * g + til
                    buf = x0t[til]
                    if g >= 1:
                        lx4[til].after(C.V, x0_used[(g - 1, til)])
                        lx4[til].after(sx4[til].sem, x1_last[(g - 1, til)])
                    c_ld = lx4[til].start(buf[:, :], x0[128 * ti:128 * (ti + 1), :])
                    V.after(lx4[til].sem, c_ld)
                    S.after(lx4[til].sem, c_ld)
                    prev_rd, c_lastev = ln_transpose_tile(
                        C, nc, buf, xnb,
                        lambda s, _t=til: xfm[:, 512 * s + 128 * _t:512 * s + 128 * (_t + 1)],
                        L, sident, ptr,
                        prev_rd if til > 0 else (C.T.sem.n if g >= 1 else None))
                t.wait_ge(C.S.sem.h, c_lastev)
                if g == 0:
                    t.wait_ge(GD.sem.h, GD.sem.n)
                V.after(lrope.sem, c_rt)
                rope_done = {}
                pevac = {}
                for i in range(16):
                    s, isk = divmod(i, 2)
                    wmat, dst = ((wq, Qr), (wk, Kr))[isk]
                    pb = pk[i % 4]
                    if i >= 4:
                        t.wait_ge(C.S.sem.h, pevac[i - 4])
                    for kc in range(8):
                        ins = t.matmul(pb[:, :], wmat[kc][:, 128 * s:128 * (s + 1)],
                                       xfm[:, 512 * kc:512 * (kc + 1)],
                                       start=(kc == 0), stop=(kc == 7))
                    c_mm = C.T.sem.inc(ins)
                    S.after(C.T, c_mm)
                    if i >= 2:
                        S.after(C.V, rope_done[i - 2])
                    pevac[i] = S.activation(qraw[i % 2][:, :], pb[:, :], AF.Silu)
                    V.after(C.S, pevac[i])
                    q_ = qraw[i % 2]
                    V.tensor_tensor(rtmp[:, :], q_[:, :], scos[g % 2][:, :], ALU.mult)
                    for (dpo, spo) in ((0, 32), (32, 0), (64, 96), (96, 64)):
                        V.tensor_copy(rcr[dpo:dpo + 32, :], q_[spo:spo + 32, :])
                    V.tensor_tensor(rcr[:, :], rcr[:, :], snsin[g % 2][:, :], ALU.mult)
                    rope_done[i] = V.tensor_tensor(dst[:, 512 * s:512 * (s + 1)],
                                                   rtmp[:, :], rcr[:, :], ALU.add)
                t.wait_ge(C.V.sem.h, rope_done[15])
                ev = {}
                for s in range(SLAB):
                    for til in range(4):
                        j = 4 * s + til
                        pt = ptr[j % 2]
                        if j >= 2:
                            t.wait_ge(C.S.sem.h, ev[j - 2])
                        c_tr = T.transpose(pt, Kr[:, 512 * s + 128 * til:512 * s + 128 * (til + 1)],
                                           sident[:, :])
                        S.after(C.T, c_tr)
                        ev[j] = S.copy(ktm[:, 1024 * til + 128 * s:1024 * til + 128 * (s + 1)], pt[:, :])
                t.wait_ge(C.S.sem.h, ev[4 * SLAB - 1])
                vp_evac = {}
                for j in range(8):
                    til, h = divmod(j, 2)
                    pb = pk[j % 4]
                    if j >= 4:
                        t.wait_ge(C.S.sem.h, vp_evac[j - 4])
                    for kc in range(8):
                        ins = t.matmul(pb[:, :], xfm[:, 512 * kc + 128 * til:512 * kc + 128 * (til + 1)],
                                       wv[kc][:, 512 * h:512 * (h + 1)],
                                       start=(kc == 0), stop=(kc == 7))
                    c_mm = C.T.sem.inc(ins)
                    S.after(C.T, c_mm)
                    vp_evac[j] = S.copy(vtm[:, 1024 * til + 512 * h:1024 * til + 512 * (h + 1)], pb[:, :])
                t.wait_ge(C.S.sem.h, vp_evac[4])
                for kc in range(8):
                    ins = t.matmul(pk[0][0:64, :], wg1[:, 64 * kc:64 * (kc + 1)],
                                   xfm[:, 512 * kc:512 * (kc + 1)],
                                   start=(kc == 0), stop=(kc == 7))
                c_mm = C.T.sem.inc(ins)
                S.after(C.T, c_mm)
                c_g1 = S.copy(g1sb[:, :], pk[0][0:64, :])
                t.wait_ge(C.S.sem.h, c_g1)
                gate_evac = {}
                for s in range(SLAB):
                    pb = pk[1 + (s % 3)]
                    if s >= 3:
                        t.wait_ge(C.S.sem.h, gate_evac[s - 3])
                    elif s == 0:
                        t.wait_ge(C.S.sem.h, vp_evac[7])
                    ins = t.matmul(pb[:, :], wg2[:, 128 * s:128 * (s + 1)], g1sb[:, :],
                                   start=True, stop=True)
                    c_mm = C.T.sem.inc(ins)
                    S.after(C.T, c_mm)
                    gate_evac[s] = S.activation(gate[:, 512 * s:512 * (s + 1)], pk[1 + (s % 3)][:, :],
                                                AF.Sigmoid)
                c_gate_all = S.mark()
                for s in range(SLAB):
                    gcol = slice(512 * s, 512 * (s + 1))
                    mask_v = {}
                    av_t = {}
                    for kt in range(4):
                        kcol = slice(512 * s + 128 * kt, 512 * s + 128 * (kt + 1))
                        vA = slice(1024 * kt + 128 * s, 1024 * kt + 128 * s + 64)
                        vB = slice(1024 * kt + 128 * s + 64, 1024 * kt + 128 * (s + 1))
                        if kt >= 2:
                            t.wait_ge(C.V.sem.h, mask_v[kt - 2])
                        elif prev_mask_last:
                            t.wait_ge(C.V.sem.h, prev_mask_last)
                        t.matmul(px[0][:, :], Kr[0:64, kcol], Qr[0:64, gcol],
                                 start=True, stop=True, tile_position=(0, 0))
                        ins = t.matmul(px[1][:, :], Kr[64:128, kcol], Qr[64:128, gcol],
                                       start=True, stop=True, tile_position=(64, 0))
                        c_e = C.T.sem.inc(ins)
                        V.after(C.T, c_e)
                        if kt >= 1:
                            V.after(C.T, av_t[kt - 1])
                        V.tensor_tensor(eA[:, :], px[0][:, :], smask[:, 512 * kt:512 * (kt + 1)], ALU.mult)
                        mask_v[kt] = V.tensor_tensor(eB[:, :], px[1][:, :],
                                                     smask[:, 512 * kt:512 * (kt + 1)], ALU.mult)
                        if kt == 0:
                            t.wait_ge(C.S.sem.h, c_gate_all if s == 0 else attn_evac_last)
                        t.wait_ge(C.V.sem.h, mask_v[kt])
                        last_stop = (kt == 3 and g == 0)
                        t.matmul(pk[0][0:64, :], vtm[:, vA], eA[:, :],
                                 start=(kt == 0), stop=last_stop, tile_position=(0, 0))
                        ins = t.matmul(pk[1][0:64, :], vtm[:, vB], eB[:, :],
                                       start=(kt == 0), stop=last_stop, tile_position=(0, 0))
                        av_t[kt] = C.T.sem.inc(ins)
                    if g > 0:
                        t.matmul(pk[0][0:64, :], Sb[0:64, 64 * s:64 * (s + 1)],
                                 Qr[0:64, gcol], start=False, stop=True, tile_position=(0, 0))
                        ins = t.matmul(pk[1][0:64, :], Sb[64:128, 64 * s:64 * (s + 1)],
                                       Qr[64:128, gcol], start=False, stop=True,
                                       tile_position=(64, 0))
                        C.T.sem.inc(ins)
                    S.after(C.T)
                    S.copy(attn[0:64, gcol], pk[0][0:64, :])
                    attn_evac_last = S.copy(attn[64:128, gcol], pk[1][0:64, :])
                    if state_prev_slab:
                        t.wait_ge(C.V.sem.h, state_prev_slab)
                    for kt in range(4):
                        vA = slice(1024 * kt + 128 * s, 1024 * kt + 128 * s + 64)
                        t.matmul(pstate[0:64, :], ktm[:, vA], vtm[:, vA],
                                 start=(kt == 0), stop=(kt == 3), tile_position=(0, 0))
                    for kt in range(4):
                        vB = slice(1024 * kt + 128 * s + 64, 1024 * kt + 128 * (s + 1))
                        ins = t.matmul(pstate[64:128, :], ktm[:, vB], vtm[:, vB],
                                       start=(kt == 0), stop=(kt == 3), tile_position=(0, 64))
                    c_st = C.T.sem.inc(ins)
                    V.after(C.T, c_st)
                    V.tensor_tensor(Sf[:, 64 * s:64 * (s + 1)], Sf[:, 64 * s:64 * (s + 1)],
                                    pstate[:, :], ALU.add)
                    state_prev_slab = V.tensor_copy(Sb[:, 64 * s:64 * (s + 1)],
                                                    Sf[:, 64 * s:64 * (s + 1)])
                    prev_mask_last = mask_v[3]
                t.wait_ge(C.V.sem.h, state_prev_slab)
                V.after(C.S, attn_evac_last)
                c_gm = V.tensor_tensor(attn[:, :], attn[:, :], gate[:, :], ALU.mult)
                S.after(C.V, c_gm)
                c_asq = S.activation(attq[:, :], attn[:, :], AF.Square)
                t.wait_ge(C.V.sem.h, c_gm)
                t.wait_ge(C.S.sem.h, c_asq)
                for j, (src, dstrow) in enumerate(((attn, srow), (attq, sqrow))):
                    for s in range(SLAB):
                        ins = t.matmul(pk[j][0:1, :], sonesb[:, :], src[:, 512 * s:512 * (s + 1)],
                                       start=(s == 0), stop=(s == 7))
                    c_mm = C.T.sem.inc(ins)
                    V.after(C.T, c_mm)
                    V.tensor_copy(dstrow[:, :], pk[j][0:1, :])
                V.tensor_scalar(murow[:, :], srow[:, :], 1.0 / D, None, ALU.mult)
                V.tensor_scalar(sqrow[:, :], sqrow[:, :], 1.0 / D, None, ALU.mult)
                V.tensor_tensor(rsrow[:, :], murow[:, :], murow[:, :], ALU.mult)
                V.tensor_tensor(rsrow[:, :], sqrow[:, :], rsrow[:, :], ALU.subtract)
                V.tensor_scalar(rsrow[:, :], rsrow[:, :], 1.0, EPS, ALU.mult, ALU.add)
                c_v = V.reciprocal(rsrow[:, :], rsrow[:, :])
                S.after(C.V, c_v)
                c_rs = S.activation(rsrow[:, :], rsrow[:, :], AF.Sqrt)
                t.wait_ge(C.S.sem.h, c_rs)
                t.wait_ge(C.V.sem.h, C.V.sem.n)
                t.matmul(pk[2][:, :], sonesf[:, :], murow[:, :], start=True, stop=True)
                ins = t.matmul(pk[3][:, :], sonesf[:, :], rsrow[:, :], start=True, stop=True)
                c_bc = C.T.sem.inc(ins)
                V.after(C.T, c_bc)
                V.tensor_copy(mub[:, :], pk[2][:, :])
                V.tensor_copy(rsb[:, :], pk[3][:, :])
                for s in range(SLAB):
                    sl = slice(512 * s, 512 * (s + 1))
                    V.tensor_tensor(attn[:, sl], attn[:, sl], mub[:, :], ALU.subtract)
                    V.tensor_tensor(attn[:, sl], attn[:, sl], rsb[:, :], ALU.mult)
                    V.tensor_scalar(nrmd[:, sl], attn[:, sl], slnw[:, s:s + 1],
                                    slnb[:, s:s + 1], ALU.mult, ALU.add)
                c_nrmd = V.mark()
                t.wait_ge(C.V.sem.h, c_nrmd)
                for til in range(4):
                    xb = x0t[til]
                    c_r = None
                    for h in range(2):
                        pp = pk[h]
                        for kc in range(8):
                            ins = t.matmul(pp[:, :], nrmd[:, 512 * kc + 128 * til:512 * kc + 128 * (til + 1)],
                                           wo[kc][:, 512 * h:512 * (h + 1)],
                                           start=(kc == 0), stop=(kc == 7))
                        c_mm = C.T.sem.inc(ins)
                        V.after(C.T, c_mm)
                        c_r = V.tensor_tensor(xb[:, 512 * h:512 * (h + 1)], pp[:, :],
                                              xb[:, 512 * h:512 * (h + 1)], ALU.add)
                        t.wait_ge(C.V.sem.h, c_r)
                    sx4[til].after(C.V, c_r)
                    x1_last[(g, til)] = sx4[til].start(
                        x1out[128 * (4 * g + til):128 * (4 * g + til + 1), :], xb[:, :])
                    x0_used[(g, til)] = V.mark()
                rope_used[g] = V.mark()

            for q_ in sx4:
                q_.wait_all()
            for eng in (V.e, S.e, t, G.e, LD.e):
                eng.wait_ge(C.V.sem.h, C.V.sem.n)
                eng.wait_ge(C.S.sem.h, C.S.sem.n)
                eng.wait_ge(C.T.sem.h, C.T.sem.n)
                for q_ in sx4:
                    eng.wait_ge(q_.sem.h, q_.sem.n)
                eng.wait_ge(gcv.h, gcv.n)
                for gq in gcs2:
                    eng.wait_ge(gq.h, gq.n)
            pop_scope(nc)

            push_scope(nc)
            x1f = [sbuf2(nc, f"x1f{i}", [128, D], F32) for i in range(2)]
            x1w = sbuf2(nc, "x1w", [128, D], F32)
            xnb2 = sbuf2(nc, "xnb2", [128, D], BF16)
            xfm2 = sbuf2(nc, "xfm2", [128, SLAB * 1024], BF16)
            asb = sbuf2(nc, "asb", [128, 32 * 1024], BF16)
            g1f = sbuf2(nc, "g1f", [128, 32 * 1024], BF16)
            silt = sbuf2(nc, "silt", [128, 1024], BF16)
            w12b = [sbuf2(nc, f"w12b{i}", [128, 2048], BF16) for i in range(2)]
            hbuf = [sbuf2(nc, f"hb{i}", [128, 4096], BF16) for i in range(2)]
            x2t = [sbuf2(nc, f"x2t{i}", [128, D], F32) for i in range(2)]
            s1s = sbuf2(nc, "s1s", [128, 32], F32)
            s2s = sbuf2(nc, "s2s", [128, 32], F32)
            sh1s = sbuf2(nc, "sh1s", [128, 32], F32)
            sh2s = sbuf2(nc, "sh2s", [128, 32], F32)
            s3s = sbuf2(nc, "s3s", [128, D], F32)
            LD.start(s1s[:, :], s1r[:, :])
            LD.start(s2s[:, :], s2r[:, :])
            LD.start(sh1s[:, :], sh1r[:, :])
            LD.start(sh2s[:, :], sh2r[:, :])
            LD.start(s3s[:, :], s3rep[:, :])
            S.after(LD.sem)
            V.after(LD.sem)
            lw = [Dma(nc, nc.gpsimd, "lw0"), Dma(nc, nc.gpsimd, "lw1")]
            x2_last = {}
            x1_reload = [Dma(nc, nc.sync, "xr0"), Dma(nc, nc.sync, "xr1")]

            lnstate = {"x1used": {}}

            def ffn_ln_block(rt):
                prev_rd = C.T.sem.n if rt > 0 else None
                c_lastev = None
                for til in range(8):
                    ti = 8 * rt + til
                    buf = x1f[til % 2]
                    key = (rt, til - 2) if til >= 2 else (rt - 1, til + 6)
                    if key in lnstate["x1used"]:
                        lx[til % 2].after(C.V, lnstate["x1used"][key])
                    if til < 2 and rt > 0:
                        lx[til % 2].after(C.V)
                    c_ld = lx[til % 2].start(buf[:, :], x1out[128 * ti:128 * (ti + 1), :])
                    V.after(lx[til % 2].sem, c_ld)
                    S.after(lx[til % 2].sem, c_ld)
                    prev_rd, c_lastev = ln_transpose_tile(
                        C, nc, buf, xnb2,
                        lambda s, _t=til: xfm2[:, 1024 * s + 128 * _t:1024 * s + 128 * (_t + 1)],
                        L, sident, ptr, prev_rd)
                    lnstate["x1used"][(rt, til)] = C.V.sem.n
                lnstate["lastev"] = c_lastev

            for rt in range(4):
                ffn_ln_block(rt)
                t.wait_ge(C.S.sem.h, lnstate["lastev"])
                lw_done = {}
                evac_sil = {0: 0, 1: 0}
                evac_a = {0: 0, 1: 0}
                t.wait_ge(C.S.sem.h, C.S.sem.n)
                t.wait_ge(C.V.sem.h, C.V.sem.n)
                if rt == 0:
                    nc.gpsimd.wait_ge(gcs2[0].h, gcs2[0].n)
                    nc.gpsimd.wait_ge(gcs2[1].h, gcs2[1].n)
                for q_ in lw:
                    q_.e.wait_ge(q_.sem.h, q_.sem.n)
                    q_.after(C.T)
                for o in range(32):
                    wb = w12b[o % 2]
                    if o >= 2:
                        lw[o % 2].after(C.T, lw_done[o - 2])
                    lw[o % 2].start(wb[:, 0:1024], w1scr[o])
                    c_w = lw[o % 2].start(wb[:, 1024:2048], w2scr[o])
                    t.wait_ge(lw[o % 2].sem.h, c_w)
                    for th in range(2):
                        if evac_sil[th]:
                            t.wait_ge(C.S.sem.h, evac_sil[th])
                        for kc in range(8):
                            ins = t.matmul(pk[th][:, :], wb[:, 128 * kc:128 * (kc + 1)],
                                           xfm2[:, 1024 * kc + 512 * th:1024 * kc + 512 * (th + 1)],
                                           start=(kc == 0), stop=(kc == 7))
                        c1 = C.T.sem.inc(ins)
                        if evac_a[th]:
                            t.wait_ge(C.V.sem.h, evac_a[th])
                        for kc in range(8):
                            ins = t.matmul(pk[2 + th][:, :], wb[:, 1024 + 128 * kc:1024 + 128 * (kc + 1)],
                                           xfm2[:, 1024 * kc + 512 * th:1024 * kc + 512 * (th + 1)],
                                           start=(kc == 0), stop=(kc == 7))
                        c2 = C.T.sem.inc(ins)
                        S.after(C.T, c1)
                        if evac_a[th]:
                            S.after(C.V, evac_a[th])
                        evac_sil[th] = S.activation(silt[:, 512 * th:512 * (th + 1)], pk[th][:, :],
                                                    AF.Silu, scale=s1s[:, o:o + 1].opt())
                        V.after(C.S, evac_sil[th])
                        V.after(C.T, c2)
                        evac_a[th] = V.scalar_tensor_tensor(
                            asb[:, 1024 * o + 512 * th:1024 * o + 512 * (th + 1)],
                            pk[2 + th][:, :], s2s[:, o:o + 1].opt(),
                            silt[:, 512 * th:512 * (th + 1)], ALU.mult, ALU.mult)
                    lw_done[o] = C.T.sem.n
                for hscr, hs, srcb, dstb in ((h1scr, sh1s, asb, g1f), (h2scr, sh2s, g1f, asb)):
                    hb_done = {}
                    evac_g = {0: 0, 1: 0, 2: 0, 3: 0}
                    t.wait_ge(C.S.sem.h, C.S.sem.n)
                    t.wait_ge(C.V.sem.h, C.V.sem.n)
                    for q_ in lw:
                        q_.e.wait_ge(q_.sem.h, q_.sem.n)
                        q_.after(C.T)
                    for o in range(32):
                        hb = hbuf[o % 2]
                        if o >= 2:
                            lw[o % 2].after(C.T, hb_done[o - 2])
                        c_w = lw[o % 2].start(hb[:, :], hscr[o])
                        t.wait_ge(lw[o % 2].sem.h, c_w)
                        for th in range(2):
                            bank = 2 * (o % 2) + th
                            pp = pk[bank]
                            if evac_g[bank]:
                                t.wait_ge(C.S.sem.h, evac_g[bank])
                            for kc in range(32):
                                ins = t.matmul(pp[:, :], hb[:, 128 * kc:128 * (kc + 1)],
                                               srcb[:, 1024 * kc + 512 * th:1024 * kc + 512 * (th + 1)],
                                               start=(kc == 0), stop=(kc == 31))
                            c_mm = C.T.sem.inc(ins)
                            S.after(C.T, c_mm)
                            evac_g[bank] = S.activation(
                                dstb[:, 1024 * o + 512 * th:1024 * o + 512 * (th + 1)],
                                pp[:, :], AF.Gelu, scale=hs[:, o:o + 1].opt())
                        hb_done[o] = C.T.sem.n
                lw[0].after(C.T)
                lw[1].after(C.T)
                for q_ in lw:
                    q_.e.wait_ge(q_.sem.h, q_.sem.n)
                for kc in range(32):
                    lw[kc % 2].start(g1f[:, 1024 * kc:1024 * (kc + 1)], w3scr[kc])
                t.wait_ge(lw[0].sem.h, lw[0].sem.n)
                t.wait_ge(lw[1].sem.h, lw[1].sem.n)
                w3_resid = {0: 0, 1: 0, 2: 0, 3: 0}
                t.wait_ge(C.S.sem.h, C.S.sem.n)
                t.wait_ge(C.V.sem.h, C.V.sem.n)
                x1w_used = {}
                for til in range(8):
                    ti = 8 * rt + til
                    xb = x2t[til % 2]
                    xr = x1f[til % 2]
                    if til % 2 in x1w_used:
                        x1_reload[til % 2].after(C.V, x1w_used[til % 2])
                    else:
                        x1_reload[til % 2].after(C.V)
                    c_xr = x1_reload[til % 2].start(xr[:, :], x1out[128 * ti:128 * (ti + 1), :])
                    V.after(x1_reload[til % 2].sem, c_xr)
                    if til % 2 in x2_last:
                        V.after(sx[til % 2].sem, x2_last[til % 2])
                    c_r = None
                    for h in range(2):
                        bank = 2 * (til % 2) + h
                        pp = pk[bank]
                        if w3_resid[bank]:
                            t.wait_ge(C.V.sem.h, w3_resid[bank])
                        for kc in range(32):
                            ins = t.matmul(pp[:, :], asb[:, 1024 * kc + 128 * til:1024 * kc + 128 * (til + 1)],
                                           g1f[:, 1024 * kc + 512 * h:1024 * kc + 512 * (h + 1)],
                                           start=(kc == 0), stop=(kc == 31))
                        c_mm = C.T.sem.inc(ins)
                        V.after(C.T, c_mm)
                        V.tensor_tensor(xb[:, 512 * h:512 * (h + 1)], pp[:, :],
                                        s3s[:, 512 * h:512 * (h + 1)], ALU.mult)
                        c_r = V.tensor_tensor(xb[:, 512 * h:512 * (h + 1)],
                                              xb[:, 512 * h:512 * (h + 1)],
                                              xr[:, 512 * h:512 * (h + 1)], ALU.add)
                        w3_resid[bank] = c_r
                    x1w_used[til % 2] = c_r
                    sx[til % 2].after(C.V, c_r)
                    x2_last[til % 2] = sx[til % 2].start(
                        x2out[128 * ti:128 * (ti + 1), :], xb[:, :])
            sx[0].wait_all()
            sx[1].wait_all()
            pop_scope(nc)
    return nc


def prep_a_inputs(inp):
    x0 = np.asarray(inp["L0_attn_x"])
    ident = _bf16(np.eye(128, dtype=np.float32))
    mT = np.zeros((128, 4, 512), dtype=np.float32)
    for kt in range(4):
        kk = 128 * kt + np.arange(128)[:, None]
        mT[:, kt, :] = (kk <= np.arange(512)[None, :])
    maskT = _bf16(np.ascontiguousarray(mT.reshape(128, 2048)))
    onesb = _bf16(np.ones((128, 1), dtype=np.float32))
    onesf = np.ones((1, 128), dtype=np.float32)
    inv = 1.0 / (10000.0 ** (np.arange(0, 64, 2, dtype=np.float64) / 64.0))
    pos = np.arange(NTOK, dtype=np.float64)
    r = np.arange(128)
    fidx = (r % 64) % 32
    ang = pos[None, :] * inv[fidx][:, None]
    cosd = np.cos(ang).astype(np.float32)
    sind = np.sin(ang).astype(np.float32)
    sgn = np.where((r % 64) < 32, -1.0, 1.0).astype(np.float32)
    nsind = (sind * sgn[:, None]).astype(np.float32)

    def slabify(vec, nsl):
        return np.ascontiguousarray(np.asarray(vec).reshape(nsl, 128).T.astype(np.float32))

    lnw = slabify(inp["L0_attn_ln_w"], SLAB)
    lnb = slabify(inp["L0_attn_ln_b"], SLAB)
    s1r = slabify(inp["L0_ffn_w1_s"], 32)
    s2r = slabify(inp["L0_ffn_w2_s"], 32)
    sh1r = slabify(inp["L0_ffn_h1_s"], 32)
    sh2r = slabify(inp["L0_ffn_h2_s"], 32)
    s3rep = np.ascontiguousarray(np.broadcast_to(
        np.asarray(inp["L0_ffn_w3_s"])[None, :], (128, D))).astype(np.float32)
    common = dict(
        Wq=np.ascontiguousarray(np.asarray(inp["L0_attn_Wq"])),
        Wk=np.ascontiguousarray(np.asarray(inp["L0_attn_Wk"])),
        Wv=np.ascontiguousarray(np.asarray(inp["L0_attn_Wv"])),
        Wo=np.ascontiguousarray(np.asarray(inp["L0_attn_Wo"])),
        Wg1=np.ascontiguousarray(np.asarray(inp["L0_attn_Wg1"])),
        Wg2=np.ascontiguousarray(np.asarray(inp["L0_attn_Wg2"])),
        lnw=lnw, lnb=lnb, cosd=cosd, nsind=nsind,
        w1T=np.ascontiguousarray(np.asarray(inp["L0_ffn_w1_w"]).T),
        w2T=np.ascontiguousarray(np.asarray(inp["L0_ffn_w2_w"]).T),
        h1T=np.ascontiguousarray(np.asarray(inp["L0_ffn_h1_w"]).T),
        h2T=np.ascontiguousarray(np.asarray(inp["L0_ffn_h2_w"]).T),
        w3T=np.ascontiguousarray(np.asarray(inp["L0_ffn_w3_w"]).T),
        s1r=s1r, s2r=s2r, sh1r=sh1r, sh2r=sh2r, s3rep=s3rep,
        ident=ident, maskT=maskT, onesb=onesb, onesf=onesf,
    )
    maps = []
    for c in range(8):
        m = dict(common)
        m["x0"] = np.ascontiguousarray(x0[c].astype(np.float32))
        maps.append(m)
    return maps




_NC_CACHE = {}


def _get_nc(name, builder):
    if name not in _NC_CACHE:
        _NC_CACHE[name] = builder()
    return _NC_CACHE[name]


def kernel(**inputs):
    cores = list(range(8))

    nc_a = _get_nc("a", build_launch_a)
    res_a = run_bass_kernel_spmd(nc_a, prep_a_inputs(inputs), core_ids=cores)
    x2_full = np.stack([res_a.results[c]["x2"].reshape(NTOK, D) for c in range(8)])

    nc_b = _get_nc("b", build_launch_b)
    res_b = run_bass_kernel_spmd(
        nc_b, prep_b_inputs(x2_full, inputs["L1_attn_W_k"], inputs["L1_attn_W_v"],
                            inputs["L1_attn_W_q"]), core_ids=cores)
    x3 = np.empty((8, 4096, 1024), np.float32)
    amaxes = []
    for c in range(8):
        x3[:, 512 * c:512 * (c + 1), :] = res_b.results[c]["x3"].reshape(8, 512, 1024)
        amaxes.append(res_b.results[c]["amax1"].reshape(1))

    nc_c = _get_nc("c", build_launch_c)
    res_c = run_bass_kernel_spmd(
        nc_c, prep_c_inputs(x3, np.concatenate(amaxes), inputs["L1_ffn_weight"],
                            inputs["L1_ffn_bias"]), core_ids=cores)
    x4 = np.empty((8, 4096, 1024), np.float32)
    for c in range(8):
        x4[:, 512 * c:512 * (c + 1), :] = res_c.results[c]["x4"].reshape(8, 512, 1024)
    return x4
